# version 13
# baseline (speedup 1.0000x reference)
import math
import numpy as np
from contextlib import ExitStack

import concourse.bass as bass
import concourse.mybir as mybir
from concourse.bass_utils import run_bass_kernel_spmd
from concourse import aot_env

F32 = mybir.dt.float32
F32R = mybir.dt.float32r
AF = mybir.ActivationFunctionType
ALU = mybir.AluOpType
AX = mybir.AxisListType

D = 4096; H = 32; DH = 128; S = 4096; B = 2; NT = 8192
NCORE = 8; TPC = NT // NCORE
CLIP = 8.0; EPS = 1e-5; SCALE = DH ** -0.5
NDT = D // 128
QK_SLAB = 1024 * TPC
V_SLAB = TPC * 512
SLAB = QK_SLAB + V_SLAB

SLOPES = (1.0 / 2.0 ** (np.arange(1, H + 1, dtype=np.float64) * (8.0 / H)))
SLOT_CW = [64, 256, 512, 512]
SLOT_WIN = [768, 2560, 4096, 4096]
HEADS_OF = [[c, c + 8, c + 16, c + 24] for c in range(NCORE)]

ENGINES = ("sync", "scalar", "vector", "tensor", "gpsimd")

class Ctr:
    __slots__ = ("v",)
    def __init__(self): self.v = 0
    def add(self, n=1):
        self.v += n
        return self.v

def stage_inputs(ins):
    x2d = np.ascontiguousarray(
        np.asarray(ins["L0_attn_hidden_states"], np.float32).reshape(NT, D))
    Wqkv = np.asarray(ins["L0_attn_Wqkv"], np.float32)
    wqT = np.ascontiguousarray(Wqkv[0:D].T)
    wkT = np.ascontiguousarray(Wqkv[D:2 * D].T)
    vperm = np.concatenate([np.arange(128 * h, 128 * h + 128)
                            for j in range(8) for h in HEADS_OF[j]])
    wvTp = np.ascontiguousarray(Wqkv[2 * D:3 * D].T[:, vperm])
    woutTp = np.ascontiguousarray(
        np.asarray(ins["L0_attn_Wout"], np.float32).T[vperm])
    bwT = np.ascontiguousarray(np.asarray(ins["L0_ffn_base_W"], np.float32).T)
    gl = np.concatenate([
        np.asarray(ins["L0_ffn_gate_W"], np.float32).T,
        np.asarray(ins["L0_ffn_lora_A"], np.float32).reshape(224, D).T],
        axis=1)
    glw = np.ascontiguousarray(gl)
    lbR = np.ascontiguousarray(
        np.asarray(ins["L0_ffn_lora_B"], np.float32).transpose(0, 2, 1)
        .reshape(224, D))
    lnc = {k: np.asarray(ins[v], np.float32).reshape(H, 128)
           for k, v in (("qw", "L0_attn_q_ln_w"), ("qb", "L0_attn_q_ln_b"),
                        ("kw", "L0_attn_k_ln_w"), ("kb", "L0_attn_k_ln_b"))}
    i_ = np.arange(128.0)[:, None]
    j_ = np.arange(67.0)[None, :]
    alib_all = (SLOPES[:, None, None] *
                (i_[None] - (63.0 + 64.0 * j_[None]))).astype(np.float32)
    iK = np.arange(128)[:, None]
    qL = np.arange(512)[None, :]
    masks = np.stack([(iK <= qL - (64 * idx - 64)).astype(np.float32)
                      for idx in range(9)])
    ones = np.ones((128, 128), np.float32)
    ident = np.eye(128, dtype=np.float32)
    cmask = np.where(np.arange(128)[None, :] <= np.arange(128)[:, None],
                     0.0, -3.0e38).astype(np.float32)
    per_core = []
    for c in range(NCORE):
        T = slice(TPC * c, TPC * (c + 1))
        hs = HEADS_OF[c]
        alib = np.ascontiguousarray(
            alib_all[hs].transpose(1, 0, 2).reshape(128, 4 * 67))
        lncols = np.stack([v for hl in range(4) for v in
                           (lnc["qw"][hs[hl]], lnc["qb"][hs[hl]],
                            lnc["kw"][hs[hl]], lnc["kb"][hs[hl]])],
                          axis=1)
        h0 = hs[0]
        alib2d = np.tile((SLOPES[h0] * (np.arange(896.0) - 895.0))
                         .astype(np.float32).reshape(1, 896), (128, 1))
        alib2d[:, 768:896] += cmask
        per_core.append(dict(
            alib2d=alib2d,
            xfm=np.ascontiguousarray(x2d[T].T),
            xtm=np.ascontiguousarray(x2d[T]),
            wqT=wqT, wkT=wkT, wvTp=wvTp, woutTp=woutTp, bwT=bwT,
            glw=glw, lbR=lbR, alib=alib,
            lncols=np.ascontiguousarray(lncols),
            masks=np.ascontiguousarray(masks.reshape(9 * 128, 512)),
            ones=ones, ident=ident))
    return per_core

class Prog:
    def __init__(self, debug=None):
        self.debug = debug or ()
        nc = self.nc = bass.Bass("TRN2", target_bir_lowering=False, debug=False)
        self.ops = {e: [] for e in ENGINES}
        st = self.st = ExitStack()
        g = lambda n, s, k="ExternalInput", **kw: nc.dram_tensor(
            n, s, F32R, kind=k, **kw).ap()
        self.xfm = g("xfm", [D, TPC]); self.xtm = g("xtm", [TPC, D])
        self.wqT = g("wqT", [D, D]); self.wkT = g("wkT", [D, D])
        self.wvTp = g("wvTp", [D, D]); self.woutTp = g("woutTp", [D, D])
        self.bwT = g("bwT", [D, D]); self.glw = g("glw", [D, 252])
        self.lbR = g("lbR", [224, D])
        self.lncolsd = g("lncols", [128, 16])
        self.alibd = g("alib", [128, 4 * 67])
        self.masksd = g("masks", [9 * 128, 512])
        self.onesd = g("ones", [128, 128]); self.identd = g("ident", [128, 128])
        self.alib2dd = g("alib2d", [128, 896])
        self.y = g("y", [TPC, D], k="ExternalOutput")
        self.cc1q_in = g("cc1q_in", [8, QK_SLAB // 2], k="Internal")
        self.cc1q_out = g("cc1q_out", [8, QK_SLAB // 2], k="Internal",
                          addr_space="Local")
        self.cc1k_in = g("cc1k_in", [8, QK_SLAB // 2], k="Internal")
        self.cc1k_out = g("cc1k_out", [8, QK_SLAB // 2], k="Internal",
                          addr_space="Local")
        self.cc1v_in = g("cc1v_in", [8, V_SLAB], k="Internal")
        self.cc1v_out = g("cc1v_out", [8, V_SLAB], k="Internal",
                          addr_space="Local")
        self.ccs_in = g("ccs_in", [4, TPC], k="Internal")
        self.ccs_out = g("ccs_out", [32, TPC], k="Internal", addr_space="Shared")
        self.cc2a_in = g("cc2a_in", [8, 256 * TPC], k="Internal")
        self.cc2a_out = g("cc2a_out", [8, 256 * TPC], k="Internal",
                          addr_space="Local")
        self.cc2b_in = g("cc2b_in", [8, 256 * TPC], k="Internal")
        self.cc2b_out = g("cc2b_out", [8, 256 * TPC], k="Internal",
                          addr_space="Local")
        self.x2sc = g("x2sc", [TPC, D], k="Internal")
        self.ln3sc = g("ln3sc", [D, TPC], k="Internal")
        self.dbg = {}
        if "qkv" in self.debug:
            self.dbg["dqk"] = g("dqk", [8 * 1024, TPC], k="ExternalOutput")
            self.dbg["dv"] = g("dv", [8 * TPC, 512], k="ExternalOutput")
            self.dbg["dst"] = g("dst", [4, TPC], k="ExternalOutput")
        if "attn" in self.debug:
            self.dbg["dao"] = g("dao", [8 * 512, TPC], k="ExternalOutput")
        if "x2" in self.debug:
            self.dbg["dx2"] = g("dx2", [TPC, D], k="ExternalOutput")

        self.cc1qi = self.cc1q_in.rearrange("s (f t) -> s f t", t=TPC)
        self.cc1ki = self.cc1k_in.rearrange("s (f t) -> s f t", t=TPC)
        self.cc1qo = self.cc1q_out.rearrange("s (f t) -> s f t", t=TPC)
        self.cc1ko = self.cc1k_out.rearrange("s (f t) -> s f t", t=TPC)
        self.cc1i_v = self.cc1v_in.rearrange("s (t f) -> s t f", f=512)
        self.cc1o_v = self.cc1v_out.rearrange("s (t f) -> s t f", f=512)
        self.cc2ai = self.cc2a_in.rearrange("s (f t) -> s f t", t=TPC)
        self.cc2ao = self.cc2a_out.rearrange("s (f t) -> s f t", t=TPC)
        self.cc2bi = self.cc2b_in.rearrange("s (f t) -> s f t", t=TPC)
        self.cc2bo = self.cc2b_out.rearrange("s (f t) -> s f t", t=TPC)
        self.ccso = self.ccs_out.rearrange("(s r) t -> s r t", r=4)

        sbuf = lambda n, p, f: st.enter_context(
            nc.sbuf_tensor(n, [p, f], F32R))
        self.AR = sbuf("AR", 128, 32768)
        self.EP = sbuf("EP", 128, 4096)
        self.WRG = sbuf("WRG", 128, 6144)
        self.SV = sbuf("SV", 128, 1664)
        self.STA = st.enter_context(nc.sbuf_tensor("STA", [128, 192], F32))
        self.XT = sbuf("XT", 128, 8192)
        self.ps = [st.enter_context(nc.psum_tensor(f"ps{i}", [128, 512], F32))
                   for i in range(8)]
        self.sems = {}
        self.ctrs = {}
        for name in ("sIN", "sMM", "sEV", "sSC", "sGP", "sCC", "sOUT", "sGE",
                     "sXT0", "sXT1", "sST", *[f"sW{i}" for i in range(10)],
                     *[f"sS{i}" for i in range(12)]):
            self.sems[name] = st.enter_context(nc.semaphore(name))
            self.ctrs[name] = Ctr()

        sv = self.SV
        self.sv_lnc = sv[:, 0:16]
        self.sv_alib = sv[:, 16:284]
        self.sv_ones = sv[:, 284:412]
        self.sv_ident = sv[:, 412:540]
        self.row_mu1 = sv[0:1, 540:1564]
        self.row_rs1 = sv[32:33, 540:1564]
        self.row_stmu = sv[0:1, 540:1564]
        self.row_strs = sv[64:65, 540:1564]
        self.row_tmp = sv[32:33, 540:1052]
        self.row_tmp2 = sv[32:33, 1052:1564]
        self.row_li = [sv[64:65, 540:1052], sv[64:65, 1052:1564]]
        self.row_ap = [sv[0:1, 540 + 512 * i:1052 + 512 * i]
                       for i in range(2)] \
            + [sv[32:33, 540 + 512 * i:1052 + 512 * i] for i in range(2)]
        self.col_small = self.STA[:, 0:192]
        self.sv_masks = self.XT[:, 2048:6656]

    CHAIN = {"vector": "sEV", "scalar": "sSC", "gpsimd": "sGE"}

    def emit(self, eng, fn):
        ch = self.CHAIN.get(eng)
        if ch is None:
            self.ops[eng].append(fn)
            return None
        sem = self.sems[ch]
        idx = self.ctrs[ch].add(1)
        def wrapped(e, fn=fn, idx=idx, sem=sem):
            if idx > 1:
                e.wait_ge(sem, idx - 1)
            fn(e).then_inc(sem, 1)
        self.ops[eng].append(wrapped)
        return idx

    def emit_raw(self, eng, fn):
        self.ops[eng].append(fn)

    def dma(self, eng, out, in_, sem="sIN"):
        s = self.sems[sem]
        v = self.ctrs[sem].add(16)
        self.ops[eng].append(lambda e, o=out, i=in_, s_=s: e.dma_start(
            out=o, in_=i).then_inc(s_, 16))
        return v

    def wait(self, eng, sem, v):
        self.ops[eng].append(
            lambda e, s=self.sems[sem], vv=v: e.wait_ge(s, vv))

    def tick(self, eng, sem):
        s = self.sems[sem]
        v = self.ctrs[sem].add(1)
        return v, (lambda inst, s_=s: inst.then_inc(s_, 1))

    def tick_last(self, eng, sem):
        v, inc = self.tick(eng, sem)
        f = self.ops[eng].pop()
        self.ops[eng].append(lambda e, f=f, i=inc: i(f(e)))
        return v

    def R(self, ap):
        return ap.bitcast(F32R)

    def finalize(self):
        nc = self.nc
        with nc.allow_low_precision(
                "f32r storage everywhere; sums accumulate fp32 in-engine "
                "and round to f32r (2^-13) on store"), nc.Block() as block:
            for eng in ENGINES:
                def runner(e, ops=self.ops[eng]):
                    for f in ops:
                        f(e)
                getattr(block, eng)(runner)
        return nc

def phase_p0(P):
    d, w = P.dma, P.wait
    R = P.R
    d("sync", P.sv_lnc, P.lncolsd[:])
    d("sync", P.sv_alib, P.alibd[:])
    d("sync", P.sv_ones, P.onesd[:])
    d("sync", P.sv_ident, P.identd[:])
    P.xa = [P.AR[:, 1024 * i:1024 * (i + 1)] for i in range(NDT)]
    for i in range(NDT):
        d("sync", P.xa[i], P.xfm[128 * i:128 * (i + 1), :])
    vconsts = P.ctrs["sIN"].v

    xt = [P.XT[:, 0:4096], P.XT[:, 4096:8192]]
    sums = P.col_small[:, 0:8]
    sqs = P.col_small[:, 8:16]
    P.ev_xtm = []
    for j in range(8):
        slot = j % 2
        sem = "sXT0" if slot == 0 else "sXT1"
        if j >= 2:
            w("sync", "sEV", P.ev_xtm[j - 2])
        d("sync", xt[slot], P.xtm[128 * j:128 * (j + 1), :], sem)
        w("vector", sem, P.ctrs[sem].v)
        P.emit("vector", lambda e, o=sums[:, j:j + 1], i=xt[slot]:
               e.reduce_sum(o, i, axis=AX.X))
        v = P.emit("vector", lambda e, o=xt[slot], i=xt[slot],
                   a=sqs[:, j:j + 1]: e.scalar_tensor_tensor(
                       o, i, 1.0, i, ALU.mult, ALU.mult, accum_out=a))
        P.ev_xtm.append(v)
    mu8 = P.col_small[:, 16:24]; rs8 = P.col_small[:, 24:32]
    var8 = P.col_small[:, 32:40]
    P.emit("vector", lambda e: e.tensor_scalar_mul(mu8, sums, 1.0 / D))
    P.emit("vector", lambda e: e.scalar_tensor_tensor(
        var8, mu8, 1.0, mu8, ALU.mult, ALU.mult))
    P.emit("vector", lambda e: e.scalar_tensor_tensor(
        var8, sqs, 1.0 / D, var8, ALU.mult, ALU.subtract))
    P.emit("vector", lambda e: e.tensor_scalar_add(var8, var8, EPS))
    vv = P.emit("vector", lambda e: e.reciprocal(var8, var8))
    w("scalar", "sEV", vv)
    vs = P.emit("scalar", lambda e: e.activation(
        out=rs8, in_=var8, func=AF.Sqrt))
    w("tensor", "sIN", vconsts)
    w("tensor", "sSC", vs)
    P.emit_raw("tensor", lambda e: P.nc.tensor.transpose(
        P.ps[6][0:16, 0:128], P.col_small[:, 16:32].bitcast(F32),
        P.sv_ident.bitcast(F32)))
    vtr = P.tick_last("tensor", "sMM")
    w("vector", "sMM", vtr)
    ve = P.emit("vector", lambda e: e.tensor_copy(
        P.EP[0:16, 0:128], P.ps[6][0:16, 0:128]))
    w("gpsimd", "sEV", ve)
    for j in range(8):
        d("gpsimd", P.row_mu1[0:1, 128 * j:128 * (j + 1)],
          P.EP[j:j + 1, 0:128], "sGP")
        d("gpsimd", P.row_rs1[0:1, 128 * j:128 * (j + 1)],
          P.EP[8 + j:9 + j, 0:128], "sGP")
    vrows = P.ctrs["sGP"].v
    w("tensor", "sGP", vrows)
    for th in range(2):
        P.emit_raw("tensor", lambda e, th=th: P.nc.tensor.matmul(
            P.ps[0 + th][:, 0:512], R(P.sv_ones[0:1, 0:128]),
            R(P.row_mu1[0:1, 512 * th:512 * (th + 1)]), start=True, stop=True))
        P.emit_raw("tensor", lambda e, th=th: P.nc.tensor.matmul(
            P.ps[2 + th][:, 0:512], R(P.sv_ones[32:33, 0:128]),
            R(P.row_rs1[0:1, 512 * th:512 * (th + 1)]), start=True, stop=True))
    vbc = P.tick_last("tensor", "sMM")
    w("vector", "sMM", vbc)
    ep = [P.EP[:, 512 * i:512 * (i + 1)] for i in range(8)]
    last = None
    for i in range(NDT):
        for th in range(2):
            xi = P.xa[i][:, 512 * th:512 * (th + 1)]
            s = ep[4 + (i % 2) * 2 + th]
            P.emit("vector", lambda e, o=s, x_=xi, m=P.ps[th][:, 0:512]:
                   e.scalar_tensor_tensor(o, x_, 1.0, m, ALU.mult,
                                          ALU.subtract))
            last = P.emit("vector", lambda e, o=xi, a=s,
                          r_=P.ps[2 + th][:, 0:512]: e.tensor_mul(o, a, r_))
    P.vx_norm = last

def phase_p1(P):
    d, w = P.dma, P.wait
    R = P.R
    ep = [P.EP[:, 512 * i:512 * (i + 1)] for i in range(8)]
    wsl = [P.WRG[:, 2048 * i:2048 * (i + 1)] for i in range(3)]
    w("tensor", "sEV", P.vx_norm)

    vmm_tick = {}
    vevac = {}
    vstore = {}
    for vo in range(8):
        for q4 in range(8):
            gidx = vo * 8 + q4
            slot = gidx % 3
            sem = f"sW{slot}"
            if gidx >= 3:
                w("sync", "sMM", vmm_tick[gidx - 3])
            blk = P.wvTp[512 * q4:512 * (q4 + 1), 512 * vo:512 * (vo + 1)]
            d("sync", wsl[slot].rearrange("p (i f) -> p i f", i=4),
              blk.rearrange("(i p) f -> p i f", p=128), sem)
            w("tensor", sem, P.ctrs[sem].v)
            for dd in range(4 * q4, 4 * (q4 + 1)):
                rhs = wsl[slot][:, 512 * (dd - 4 * q4):
                                512 * (dd - 4 * q4) + 512]
                for tt in range(8):
                    if dd == 0 and vo >= 1:
                        w("tensor", "sEV", vevac[(vo - 1, tt)])
                    lhs = P.xa[dd][:, 128 * tt:128 * (tt + 1)]
                    P.emit_raw("tensor", lambda e, p=P.ps[tt][:, 0:512],
                               l=lhs, r_=rhs, st=(dd == 0),
                               sp=(dd == NDT - 1):
                               P.nc.tensor.matmul(p, R(l), R(r_),
                                                  start=st, stop=sp))
            vmm_tick[gidx] = P.tick_last("tensor", "sMM")
        w("vector", "sMM", vmm_tick[vo * 8 + 7])
        for tt in range(8):
            ssem = f"sS{tt}"
            if vo >= 1:
                w("vector", ssem, vstore[(vo - 1, tt)])
            s = ep[tt]
            vevac[(vo, tt)] = P.emit("vector", lambda e, o=s,
                                     p=P.ps[tt][:, 0:512]:
                                     e.tensor_scalar(o, p, CLIP, -CLIP,
                                                     ALU.min, ALU.max))
            w("gpsimd", "sEV", vevac[(vo, tt)])
            d("gpsimd", P.cc1i_v[vo, 128 * tt:128 * (tt + 1), :], s, ssem)
            if "qkv" in P.debug:
                dvv = P.dbg["dv"].rearrange("(s t) f -> s t f", t=TPC)
                d("gpsimd", dvv[vo, 128 * tt:128 * (tt + 1), :], s, ssem)
            vstore[(vo, tt)] = P.ctrs[ssem].v
    vcc, vinc = P.tick("gpsimd", "sCC")
    for i in range(8):
        w("gpsimd", f"sS{i}", P.ctrs[f"sS{i}"].v)
    P.emit_raw("gpsimd", lambda e, inc=vinc: inc(_cc(e, "AllToAll",
        P.cc1v_in[:], P.cc1v_out[:])))
    P.cc1v_done = vcc

    w("tensor", "sEV", vevac[(7, 7)])
    for i in range(8):
        w("vector", f"sS{i}", P.ctrs[f"sS{i}"].v)
    evac_tick = {}
    mm_half = {}
    sq_tick = {}
    store_total = {}
    for ft in range(64):
        isq = ft < 32
        src_ = P.wqT if isq else P.wkT
        fcol = 128 * (ft % 32)
        b0 = 2 * (ft % 2)
        for half in range(2):
            gidx = 2 * ft + half
            slot = gidx % 3
            sem = f"sW{slot}"
            if gidx >= 3:
                w("sync", "sMM", mm_half[gidx - 3])
            else:
                w("sync", "sMM", vmm_tick[{0: 63, 1: 61, 2: 62}[slot]])
            blk = src_[2048 * half:2048 * (half + 1), fcol:fcol + 128]
            d("sync", wsl[slot].rearrange("p (i f) -> p i f", i=16),
              blk.rearrange("(i p) f -> p i f", p=128), sem)
            w("tensor", sem, P.ctrs[sem].v)
            if half == 0 and ft >= 2:
                w("tensor", "sEV", evac_tick[ft - 2])
                w("tensor", "sMM", sq_tick[ft - 2])
            for dd in range(16 * half, 16 * (half + 1)):
                lhs = wsl[slot][:, 128 * (dd - 16 * half):
                                128 * (dd - 16 * half) + 128]
                for th in range(2):
                    P.emit_raw("tensor", lambda e, p=P.ps[b0 + th][:, 0:512],
                               l=lhs, r_=P.xa[dd][:, 512 * th:512 * (th + 1)],
                               st=(dd == 0), sp=(dd == NDT - 1):
                               P.nc.tensor.matmul(p, R(l), R(r_),
                                                  start=st, stop=sp))
            mm_half[gidx] = P.tick_last("tensor", "sMM")
        w("vector", "sMM", mm_half[2 * ft + 1])
        if ft >= 2:
            for ssem, sval in store_total[ft - 2].items():
                w("vector", ssem, sval)
            w("vector", "sSC", P.ctrs["sSC"].v)
        eA = [ep[0 + (ft % 2)], ep[2 + (ft % 2)]]
        eB = [ep[4 + (ft % 2)], ep[6 + (ft % 2)]]
        for th in range(2):
            evac_tick[ft] = P.emit("vector", lambda e, o=eA[th],
                                   p=P.ps[b0 + th][:, 0:512]:
                                   e.tensor_scalar(o, p, CLIP, -CLIP,
                                                   ALU.min, ALU.max))
        w("scalar", "sEV", evac_tick[ft])
        if ft >= 2:
            w("scalar", "sMM", sq_tick[ft - 2])
        vsq = None
        for th in range(2):
            vsq = P.emit("scalar", lambda e, o=eB[th], i=eA[th]:
                         e.activation(out=o, in_=i, func=AF.Square))
        w("tensor", "sSC", vsq)
        first = ft in (0, 32)
        last = ft in (31, 63)
        for th in range(2):
            P.emit_raw("tensor", lambda e, p=P.ps[4 + 2 * th][0:1, 0:512],
                       r_=eA[th], st=first, sp=last: P.nc.tensor.matmul(
                           p, R(P.sv_ones[:, 0:1]), R(r_), start=st, stop=sp))
            P.emit_raw("tensor", lambda e, p=P.ps[5 + 2 * th][0:1, 0:512],
                       r_=eB[th], st=first, sp=last: P.nc.tensor.matmul(
                           p, R(P.sv_ones[:, 0:1]), R(r_), start=st, stop=sp))
        sq_tick[ft] = P.tick_last("tensor", "sMM")
        w("gpsimd", "sEV", evac_tick[ft])
        h = ft if isq else ft - 32
        r0 = 128 * (h // 8)
        tgt = P.cc1qi if isq else P.cc1ki
        sts = {}
        for th in range(2):
            ssem = f"sS{(ft % 2) * 2 + th}"
            d("gpsimd", tgt[h % 8, r0:r0 + 128,
                            512 * th:512 * (th + 1)], eA[th], ssem)
            if "qkv" in P.debug:
                dv = P.dbg["dqk"].rearrange("(s f) t -> s f t", f=1024)
                d("gpsimd", dv[h % 8, (0 if isq else 512) + r0:
                               (0 if isq else 512) + r0 + 128,
                               512 * th:512 * (th + 1)], eA[th], ssem)
            sts[ssem] = P.ctrs[ssem].v
        store_total[ft] = sts
        if last:
            w("vector", "sMM", sq_tick[ft])
            w("vector", "sST", P.ctrs["sST"].v)
            base = 0 if isq else 2
            vs2 = None
            for th in range(2):
                pS = P.ps[4 + 2 * th][0:1, 0:512]
                pQ = P.ps[5 + 2 * th][0:1, 0:512]
                murow = P.row_stmu[0:1, 512 * th:512 * (th + 1)]
                rsrow = P.row_strs[0:1, 512 * th:512 * (th + 1)]
                tmp = P.row_tmp
                tmp2 = P.row_tmp2
                if th == 1:
                    w("vector", "sSC", vs2)
                P.emit("vector", lambda e, o=murow, i=pS:
                       e.tensor_scalar_mul(o, i, 1.0 / D))
                P.emit("vector", lambda e, o=tmp, i=pQ:
                       e.tensor_scalar_mul(o, i, 1.0 / D))
                P.emit("vector", lambda e, o=tmp2, m=murow:
                       e.tensor_mul(o, m, m))
                P.emit("vector", lambda e, o=tmp, a=tmp, b=tmp2:
                       e.tensor_sub(o, a, b))
                P.emit("vector", lambda e, o=tmp:
                       e.tensor_scalar_add(o, o, EPS))
                vv = P.emit("vector", lambda e, o=tmp: e.reciprocal(o, o))
                w("scalar", "sEV", vv)
                vs2 = P.emit("scalar", lambda e, o=rsrow, i=tmp:
                             e.activation(out=o, in_=i, func=AF.Sqrt))
            w("tensor", "sSC", vs2)
            w("gpsimd", "sSC", vs2)
            d("gpsimd", P.ccs_in[base:base + 1, :], P.row_stmu, "sST")
            d("gpsimd", P.ccs_in[base + 1:base + 2, :], P.row_strs, "sST")
            if "qkv" in P.debug:
                d("gpsimd", P.dbg["dst"][base:base + 1, :], P.row_stmu, "sST")
                d("gpsimd", P.dbg["dst"][base + 1:base + 2, :], P.row_strs,
                  "sST")
        if ft == 31:
            vq, incq = P.tick("gpsimd", "sCC")
            for i in range(4):
                w("gpsimd", f"sS{i}", P.ctrs[f"sS{i}"].v)
            P.emit_raw("gpsimd", lambda e, inc=incq: inc(_cc(e, "AllToAll",
                P.cc1q_in[:], P.cc1q_out[:])))
            P.cc1q_done = vq

    v2, inc2 = P.tick("gpsimd", "sCC")
    w("gpsimd", "sST", P.ctrs["sST"].v)
    P.emit_raw("gpsimd", lambda e, inc=inc2: inc(_cc(e, "AllGather",
        P.ccs_in[:], P.ccs_out[:])))
    P.ccs_done = v2
    v3, inc3 = P.tick("gpsimd", "sCC")
    for i in range(4):
        w("gpsimd", f"sS{i}", P.ctrs[f"sS{i}"].v)
    P.emit_raw("gpsimd", lambda e, inc=inc3: inc(_cc(e, "AllToAll",
        P.cc1k_in[:], P.cc1k_out[:])))
    P.cc1k_done = v3

def _cc(e, kind, in_, out):
    with aot_env.allow_ici_all_to_all():
        return e.collective_compute(
            kind, ALU.bypass, ins=[in_], outs=[out],
            replica_groups=[[0, 1, 2, 3, 4, 5, 6, 7]])

def phase_p2(P):
    d, w, t = P.dma, P.wait, P.tick
    w("sync", "sEV", P.ctrs["sEV"].v)
    d("sync", P.sv_masks.rearrange("p (m q) -> p m q", m=9),
      P.masksd.rearrange("(m p) q -> p m q", p=128), "sW5")
    P.masks_done = ("sW5", P.ctrs["sW5"].v)



def _emit_slot0_job(P, job_i, b, s, arK, arQ, arV, pw_at):
    d, w = P.dma, P.wait
    R = P.R
    win = SLOT_WIN[0]
    if P._norm_pend is not None:
        _emit_norm(P, arK, arQ)
    if P._st_now.get(0) is not None:
        w("vector", "sS10", P._st_now[0])
    if P._st_now.get(1) is not None:
        w("vector", "sS11", P._st_now[1])
    w("tensor", "sEV", P.ctrs["sEV"].v)
    w("tensor", "sSC", P.ctrs["sSC"].v)
    w("tensor", *P.p3_rows_done)
    w("vector", *P.p3_rows_done)
    msl = [P.EP[:, 0:896], P.EP[:, 1024:1920]]
    ptt = [P.EP[:, 2048:2944], P.EP[:, 3072:3968]]
    osl = [P.XT[:, 1024:1152], P.XT[:, 1152 + 128:1152 + 256]]
    negm = [P.col_small[:, 106:107], P.col_small[:, 107:108]]
    lcol = [P.col_small[:, 108:109], P.col_small[:, 109:110]]
    ab = P.XT[:, 6656:7552]
    for QT in range(32):
        for fn in pw_at.get(QT, []):
            fn()
        par = QT % 2
        q0 = 128 * QT
        kl = max(0, (q0 - win + 1)) // 128
        kh = QT
        nk = kh - kl + 1
        width = 128 * nk
        if P._s0_scores is not None:
            w("tensor", "sEV", P._s0_scores)
        idx0 = 128 * (kl - QT) + 768
        nch = (width + 511) // 512
        for cchunk in range(nch):
            c0 = 512 * cchunk
            cN = min(512, width - c0)
            P.emit_raw("tensor", lambda e, p=P.ps[cchunk][:, 0:cN], c0=c0,
                       l=arQ[s][:, q0:q0 + 128],
                       r_=arK[s][:, 128 * kl + c0:128 * kl + c0 + cN]:
                       P.nc.tensor.matmul(p, R(l), R(r_),
                                          start=True, stop=True))
        vqk = P.tick_last("tensor", "sMM")
        if P._s0_tpend is not None:
            P._s0_tpend()
            P._s0_tpend = None
        w("vector", "sMM", vqk)
        if P._s0_msc.get(par) is not None:
            w("vector", "sSC", P._s0_msc[par])
        if P._s0_ptt.get(par) is not None:
            w("vector", "sMM", P._s0_ptt[par])
        vlast = None
        for cb in range(nch):
            lo = 512 * cb
            hi = min(lo + 512, width)
            vlast = P.emit("vector", lambda e, o=msl[par][:, lo:hi],
                           p=P.ps[cb][:, 0:hi - lo],
                           m=ab[:, idx0 + lo:idx0 + hi]:
                           e.scalar_tensor_tensor(o, p, SCALE, m,
                                                  ALU.mult, ALU.add))
        P._s0_scores = vlast
        P.emit("vector", lambda e, o=negm[par], i=msl[par][:, 0:width]:
               e.reduce_max(o, i, axis=AX.X))
        vng = P.emit("vector", lambda e, o=negm[par]:
                     e.tensor_scalar_mul(o, o, -1.0))
        w("scalar", "sEV", vng)
        vex = P.emit("scalar", lambda e, o=msl[par][:, 0:width],
                     i=msl[par][:, 0:width], bias=negm[par], a=lcol[par]:
                     e.activation(out=o, in_=i, func=AF.Exp, bias=bias,
                                  accum_out=a))
        w("vector", "sSC", vex)
        vrc = P.emit("vector", lambda e, o=lcol[par]: e.reciprocal(o, o))
        if P._s0_vpend is not None:
            P._s0_vpend()
            P._s0_vpend = None
        w("scalar", "sEV", vrc)
        vpn = P.emit("scalar", lambda e, o=msl[par][:, 0:width],
                     i=msl[par][:, 0:width], m=lcol[par]:
                     e.activation(out=o, in_=i, func=AF.Identity, scale=m))
        P._s0_msc[par] = vpn

        def mkT(par=par, nk=nk, kl=kl, s=s, q0=q0, b=b, vpn=vpn):
            pendPV = None
            for blk in range(nk):
                tb = 2 + blk % 2
                if P._s0_trc.get(blk % 2) is not None:
                    w("tensor", "sSC", P._s0_trc[blk % 2])
                if blk == 0:
                    w("tensor", "sSC", vpn)
                P.emit_raw("tensor", lambda e, p=P.ps[tb][:, 0:128],
                           i=msl[par][:, 128 * blk:128 * (blk + 1)]:
                           P.nc.tensor.transpose(p, i.bitcast(F32),
                                                 P.sv_ident.bitcast(F32)))
                vtr = P.tick_last("tensor", "sMM")
                w("scalar", "sMM", vtr)
                vtc = P.emit("scalar", lambda e, o=ptt[par][:, 128 * blk:
                             128 * (blk + 1)], p=P.ps[tb][:, 0:128]:
                             e.activation(out=o, in_=p, func=AF.Copy))
                P._s0_trc[blk % 2] = vtc
                if pendPV is not None:
                    pendPV()
                def mkPV(blk=blk, vtc=vtc, st=(blk == 0),
                         sp=(blk == nk - 1)):
                    if st and P._s0_pv.get(par) is not None:
                        w("tensor", "sEV", P._s0_pv[par])
                    w("tensor", "sSC", vtc)
                    P.emit_raw("tensor", lambda e,
                               p=P.ps[6 + par][:, 0:128],
                               l=arV[s][:, 128 * (kl + blk):
                                        128 * (kl + blk + 1)],
                               r_=ptt[par][:, 128 * blk:128 * (blk + 1)],
                               st=st, sp=sp:
                               P.nc.tensor.matmul(p, R(l), R(r_),
                                                  start=st, stop=sp))
                    P._s0_ptt[par] = P.tick_last("tensor", "sMM")
                pendPV = mkPV
            pendPV()
            lastpv = P._s0_ptt[par]

            def vos_fn(par=par, q0=q0, b=b, lastpv=lastpv):
                w("vector", "sMM", lastpv)
                osem = f"sS{10 + par}"
                if P._st_now.get(par) is not None:
                    w("vector", osem, P._st_now[par])
                vos = P.emit("vector", lambda e, o=osl[par],
                             p=P.ps[6 + par][:, 0:128]: e.tensor_copy(o, p))
                P._s0_pv[par] = vos
                w("gpsimd", "sEV", vos)
                gtok = b * S + q0
                slab = gtok // 1024
                col = gtok % 1024
                d("gpsimd", P.cc2ai[slab, 0:128, col:col + 128], osl[par],
                  osem)
                if "attn" in P.debug:
                    dao = P.dbg["dao"].rearrange("(s f) t -> s f t", f=512)
                    d("gpsimd", dao[slab, 0:128, col:col + 128], osl[par],
                      osem)
                P._st_now[par] = P.ctrs[osem].v
            P._s0_vpend = vos_fn
        P._s0_tpend = mkT
    P._s0_tpend()
    P._s0_tpend = None
    if P._s0_vpend is not None:
        P._s0_vpend()
        P._s0_vpend = None

def _emit_norm(P, arK, arQ):
    d, w = P.dma, P.wait
    R = P.R
    ep = [P.EP[:, 512 * i:512 * (i + 1)] for i in range(8)]
    job_i, b, hl, J, cw, q0, outb, vli, stsl = P._norm_pend
    P._norm_pend = None
    li = P.row_li[J % 2][0:1, 0:cw]
    w("tensor", "sEV", vli)
    P.emit_raw("tensor", lambda e, o=P.ps[outb][:, 0:cw], li=li:
               P.nc.tensor.matmul(o, R(P.sv_ones[64:65, 0:128]), R(li),
                                  start=True, stop=True))
    vlb = P.tick_last("tensor", "sMM")
    P._norm_vlb[J % 2] = vlb
    w("vector", "sMM", vlb)
    os_ = ep[4 + J % 2]
    osem = f"sS{10 + J % 2}"
    if P._st_now.get(J % 2) is not None:
        w("vector", osem, P._st_now[J % 2])
    vn = P.emit("vector", lambda e, o=os_[:, 0:cw], a=stsl,
                p=P.ps[outb][:, 0:cw]: e.tensor_mul(o, a, p))
    P._norm_done[J % 2] = vn
    w("gpsimd", "sEV", vn)
    gtok = b * S + q0
    slab = gtok // 1024
    col = gtok % 1024
    tgt = P.cc2ai if hl < 2 else P.cc2bi
    d("gpsimd", tgt[slab, 128 * (hl % 2):128 * (hl % 2) + 128, col:col + cw],
      os_[:, 0:cw], osem)
    if "attn" in P.debug:
        dao = P.dbg["dao"].rearrange("(s f) t -> s f t", f=512)
        d("gpsimd", dao[slab, 128 * hl:128 * (hl + 1), col:col + cw],
          os_[:, 0:cw], osem)
    P._st_now[J % 2] = P.ctrs[osem].v

def phase_p3(P):
    d, w = P.dma, P.wait
    R = P.R
    ep = [P.EP[:, 512 * i:512 * (i + 1)] for i in range(8)]
    JOB = 12288
    arK = [P.AR[:, JOB * s:JOB * s + 4096] for s in range(2)]
    arQ = [P.AR[:, JOB * s + 4096:JOB * s + 8192] for s in range(2)]
    arV = [P.AR[:, JOB * s + 8192:JOB * s + 12288] for s in range(2)]
    jobs = [(b, hl) for b in range(B) for hl in range(4)]
    w("sync", "sCC", P.cc1v_done)
    w("gpsimd", "sCC", P.ccs_done)
    w("gpsimd", *P.masks_done)
    w("tensor", "sEV", P.ctrs["sEV"].v)
    w("tensor", "sSC", P.ctrs["sSC"].v)
    w("sync", "sMM", P.ctrs["sMM"].v)
    d("sync", P.XT[:, 6656:7552], P.alib2dd[:], "sW6")
    P.p3_rows_done = ("sW6", P.ctrs["sW6"].v)
    P._av_load = {}

    def load_job(job_i):
        b, hl = jobs[job_i]
        s = job_i % 2
        sem = f"sW{3 + s}"
        vsem = f"sW{7 + s}"
        for j in range(4):
            vv = P.cc1o_v[4 * b + j].rearrange("(tt p) f -> tt p f", p=128)
            for tt in range(8):
                d("sync", arV[s][:, 128 * (8 * j + tt):128 * (8 * j + tt + 1)],
                  vv[tt, :, 128 * hl:128 * (hl + 1)], vsem)
        P._av_load[job_i] = P.ctrs[vsem].v
        if job_i == 0:
            w("sync", "sCC", P.cc1k_done)
        for j in range(4):
            d("sync", arK[s][:, 1024 * j:1024 * (j + 1)],
              P.cc1ko[4 * b + j, 128 * hl:128 * (hl + 1), :], sem)
            d("sync", arQ[s][:, 1024 * j:1024 * (j + 1)],
              P.cc1qo[4 * b + j, 128 * hl:128 * (hl + 1), :], sem)
        return P.ctrs[sem].v

    def apply_chunk(job_i, ac):
        b, hl = jobs[job_i]
        s = job_i % 2
        isk = ac < 8
        ch = ac % 8
        ar = arK[s] if isk else arQ[s]
        mu_r, rs_r = (2, 3) if isk else (0, 1)
        w_ = P.sv_lnc[:, 4 * hl + (2 if isk else 0):
                      4 * hl + (3 if isk else 1)].bitcast(F32)
        b_ = P.sv_lnc[:, 4 * hl + (3 if isk else 1):
                      4 * hl + (4 if isk else 2)].bitcast(F32)
        par = ac % 2
        mu_row = P.row_ap[par]
        rs_row = P.row_ap[2 + par]
        sl = 4 * b + ch // 2
        half = 512 * (ch % 2)
        rsem = f"sS{8 + par}"
        if P._ap_vb.get(par) is not None:
            w("gpsimd", "sMM", P._ap_vb[par])
        d("gpsimd", mu_row, P.ccso[sl, mu_r, half:half + 512], rsem)
        d("gpsimd", rs_row, P.ccso[sl, rs_r, half:half + 512], rsem)
        vrows = (rsem, P.ctrs[rsem].v)
        prev = P._ap_tick.get((job_i, ac - 1) if ac else (job_i - 1, 15))
        if prev is not None:
            w("tensor", "sEV", prev)
        w("tensor", *vrows)
        P.emit_raw("tensor", lambda e, r_=mu_row: P.nc.tensor.matmul(
            P.ps[4][:, 0:512], R(P.sv_ones[0:1, 0:128]), R(r_),
            start=True, stop=True))
        P.emit_raw("tensor", lambda e, r_=rs_row: P.nc.tensor.matmul(
            P.ps[5][:, 0:512], R(P.sv_ones[32:33, 0:128]), R(r_),
            start=True, stop=True))
        vb = P.tick_last("tensor", "sMM")
        P._ap_vb[par] = vb
        w("vector", "sMM", vb)
        if P._ap_load.get(job_i) is not None:
            w("vector", f"sW{3 + s}", P._ap_load[job_i])
            P._ap_load[job_i] = None
        x_ = ar[:, 512 * ch:512 * (ch + 1)]
        esl = P.XT[:, 512 * par:512 * (par + 1)]
        P.emit("vector", lambda e, o=esl, i=x_:
               e.scalar_tensor_tensor(o, i, 1.0, P.ps[4][:, 0:512],
                                      ALU.mult, ALU.subtract))
        P.emit("vector", lambda e, o=x_, i=esl, w__=w_:
               e.scalar_tensor_tensor(o, i, w__, P.ps[5][:, 0:512],
                                      ALU.mult, ALU.mult))
        va = P.emit("vector", lambda e, o=x_, b__=b_:
                    e.tensor_scalar_add(o, o, b__))
        P._ap_tick[(job_i, ac)] = va

    P._ap_tick = {}
    P._ap_vb = {}
    P._ap_load = {}
    P._norm_pend = None
    P._norm_done = {}
    P._norm_vlb = {}
    P._st_now = {}
    P._s0_msc = {}
    P._s0_ptt = {}
    P._s0_trc = {}
    P._s0_pv = {}
    P._s0_st = {}
    P._s0_scores = None
    P._s0_tpend = None
    P._s0_vpend = None
    P._pv_pend = None
    P.job_mm_done = {}
    st_now = {}
    pt_tick = {}
    exp_tick = {}
    ci = 0

    P._ap_load[0] = load_job(0)
    for ac in range(16):
        apply_chunk(0, ac)

    for job_i, (b, hl) in enumerate(jobs):
        s = job_i % 2
        cw = SLOT_CW[hl]; win = SLOT_WIN[hl]
        nJ = 32 if hl == 0 else S // cw
        prework = []
        if job_i + 1 < len(jobs):
            def start_next(job_n=job_i + 1, cur=job_i):
                if cur >= 1:
                    w("sync", "sMM", P.job_mm_done[cur - 1])
                P._ap_load[job_n] = load_job(job_n)
            prework.append(start_next)
            for ac in range(16):
                prework.append(lambda job_n=job_i + 1, ac=ac:
                               apply_chunk(job_n, ac))
        pw_at = {}
        if prework:
            span = max(1, nJ - 2)
            for ii, fn in enumerate(prework):
                at = 1 + (ii * span) // len(prework)
                pw_at.setdefault(min(at, nJ - 1), []).append(fn)

        w("tensor", "sEV", P._ap_tick[(job_i, 15)])
        w("tensor", f"sW{7 + job_i % 2}", P._av_load[job_i])
        stash = P.AR[:, 24576 + 4096 * (job_i % 2):24576 + 4096 * (job_i % 2)
                     + 4096]
        if hl == 0:
            _emit_slot0_job(P, job_i, b, s, arK, arQ, arV, pw_at)
            P.job_mm_done[job_i] = P.ctrs["sMM"].v
            P._s0_done = P.ctrs["sEV"].v
            P._s0_done_sc = P.ctrs["sSC"].v
            continue
        if getattr(P, "_s0_done", None) is not None:
            w("tensor", "sEV", P._s0_done)
            w("tensor", "sSC", P._s0_done_sc)
            P._s0_done = None
        if job_i == 7:
            vca, inca = P.tick("gpsimd", "sCC")
            w("gpsimd", "sS10", P.ctrs["sS10"].v)
            w("gpsimd", "sS11", P.ctrs["sS11"].v)
            P.emit_raw("gpsimd", lambda e, inc=inca: inc(_cc(e, "AllToAll",
                P.cc2a_in[:], P.cc2a_out[:])))
            P.cc2a_done = vca
        for J in range(nJ):
            for fn in pw_at.get(J, []):
                fn()
            k0 = cw * (J + 1) - 1
            kc_hi = k0 // 128
            kc_lo = max(0, (k0 + 1 - win) // 128)
            q0 = cw * J
            outb = 2 if J % 2 == 0 else 6
            lb_ = 3 if J % 2 == 0 else 7
            for kc in range(kc_lo, kc_hi + 1):
                sb_ = ci % 2
                if ci >= 2:
                    w("tensor", "sSC", exp_tick[ci - 2])
                P.emit_raw("tensor", lambda e, p=P.ps[sb_][:, 0:cw],
                           l=arK[s][:, 128 * kc:128 * (kc + 1)],
                           r_=arQ[s][:, q0:q0 + cw]:
                           P.nc.tensor.matmul(p, R(l), R(r_),
                                              start=True, stop=True))
                vqk = P.tick_last("tensor", "sMM")
                if kc == kc_lo + 1 and P._norm_pend is not None:
                    _emit_norm(P, arK, arQ)
                shift = k0 - 128 * kc
                ji = (shift - 63) // 64
                assert (shift - 63) % 64 == 0 and 0 <= ji < 67, (shift, ji)
                pts = ep[ci % 4]
                w("scalar", "sMM", vqk)
                if ci >= 4:
                    w("scalar", "sMM", pt_tick[ci - 4])
                vex = P.emit("scalar", lambda e, o=pts[:, 0:cw],
                             p=P.ps[sb_][:, 0:cw],
                             bias=P.sv_alib[:, 67 * hl + ji:67 * hl + ji + 1]:
                             e.activation(out=o, in_=p, func=AF.Exp,
                                          bias=bias, scale=SCALE))
                exp_tick[ci] = vex
                ptready = ("sSC", vex)
                sshift = 128 * kc - q0
                if sshift > -128:
                    midx = sshift // 64 + 1
                    assert 0 <= midx < 9, (sshift, midx)
                    w("vector", "sSC", vex)
                    vm = P.emit("vector", lambda e, o=pts[:, 0:cw],
                                m=P.sv_masks[:, 512 * midx:512 * midx + cw]:
                                e.tensor_mul(o, o, m))
                    ptready = ("sEV", vm)
                if P._pv_pend is not None:
                    P._pv_pend()
                def mkC(ci=ci, pts=pts, ptready=ptready,
                        first=(kc == kc_lo), sp=(kc == kc_hi), kc=kc,
                        outb=outb, lb_=lb_, cw=cw, Jp=J % 2, s=s):
                    if first and P._norm_done.get(Jp) is not None:
                        w("tensor", "sEV", P._norm_done[Jp])
                    w("tensor", *ptready)
                    P.emit_raw("tensor", lambda e, p=P.ps[outb][:, 0:cw],
                               l=arV[s][:, 128 * kc:128 * (kc + 1)],
                               r_=pts[:, 0:cw], st=first, sp=sp:
                               P.nc.tensor.matmul(p, R(l), R(r_),
                                                  start=st, stop=sp))
                    P.emit_raw("tensor", lambda e, p=P.ps[lb_][0:1, 0:cw],
                               r_=pts[:, 0:cw], st=first, sp=sp:
                               P.nc.tensor.matmul(p, R(P.sv_ones[:, 0:1]),
                                                  R(r_), start=st, stop=sp))
                    pt_tick[ci] = P.tick_last("tensor", "sMM")
                P._pv_pend = mkC
                ci += 1
            P._pv_pend()
            P._pv_pend = None
            w("vector", "sMM", pt_tick[ci - 1])
            P.emit("vector", lambda e, o=stash[:, q0:q0 + cw],
                   p=P.ps[outb][:, 0:cw]: e.tensor_copy(o, p))
            li = P.row_li[J % 2][0:1, 0:cw]
            if P._norm_vlb.get(J % 2) is not None:
                w("vector", "sMM", P._norm_vlb[J % 2])
            P.emit("vector", lambda e, o=li, p=P.ps[lb_][0:1, 0:cw]:
                   e.tensor_scalar_max(o, p, 1e-30))
            vli = P.emit("vector", lambda e, o=li: e.reciprocal(o, o))
            if P._norm_pend is not None:
                _emit_norm(P, arK, arQ)
            P._norm_pend = (job_i, b, hl, J, cw, q0, outb, vli,
                            stash[:, q0:q0 + cw])
        P.job_mm_done[job_i] = pt_tick[ci - 1]
        if job_i == len(jobs) - 1:
            _emit_norm(P, arK, arQ)
    P.p3_store_total =    P.p3_store_total = (P.ctrs["sS10"].v, P.ctrs["sS11"].v)

def phase_p4(P):
    w, t = P.wait, P.tick
    w("gpsimd", "sS10", P.p3_store_total[0])
    w("gpsimd", "sS11", P.p3_store_total[1])
    v1, inc1 = t("gpsimd", "sCC")
    P.emit_raw("gpsimd", lambda e, inc=inc1: inc(_cc(e, "AllToAll",
        P.cc2b_in[:], P.cc2b_out[:])))
    P.cc2b_done = v1

P5SEQ = [(j_, h_) for h_ in (0, 1) for j_ in range(8)] + \
        [(j_, h_) for h_ in (2, 3) for j_ in range(8)]

def phase_p56(P, which):
    d, w = P.dma, P.wait
    R = P.R
    P._trev = {}
    P._trgp = {}
    ep = [P.EP[:, 512 * i:512 * (i + 1)] for i in range(8)]
    is5 = which == 5
    wsrc = P.woutTp if is5 else P.bwT
    afm = [P.AR[:, 512 * i:512 * (i + 1)] for i in range(NDT)]
    stg = [P.AR[:, 16384 + 4096 * tt:16384 + 4096 * (tt + 1)]
           for tt in range(4)]
    junkof = [P.AR[:, 4096 * i:4096 * (i + 1)] for i in range(4)]
    STA = P.col_small
    hst = [P.WRG[:, 5120 + 256 * tt:5120 + 256 * (tt + 1)] for tt in range(4)]
    wslot = [P.WRG[:, 512 * i:512 * (i + 1)] for i in range(10)]
    bank = {}
    bank_sc = {}
    wtick = {}
    gtick = {}
    hst_tick = {}
    w("sync", "sMM", P.ctrs["sMM"].v)
    w("sync", "sEV", P.ctrs["sEV"].v)
    w("tensor", "sEV", P.ctrs["sEV"].v)
    w("tensor", "sSC", P.ctrs["sSC"].v)
    w("tensor", "sGE", P.ctrs["sGE"].v)
    if not is5:
        for ssem, sval in P.p5_store_total.items():
            w("sync", ssem, sval)
    widx = 0
    gwidx = 0
    for tsup in range(2):
        tks = slice(512 * tsup, 512 * (tsup + 1))
        if tsup == 1:
            w("sync", "sMM", P.ctrs["sMM"].v)
            w("sync", "sEV", P.ctrs["sEV"].v)
        vload16 = None
        if is5:
            if tsup == 0:
                w("sync", "sCC", P.cc2a_done)
            for i_, (j_, h_) in enumerate(P5SEQ):
                if i_ == 16:
                    vload16 = P.ctrs["sIN"].v
                    if tsup == 0:
                        w("sync", "sCC", P.cc2b_done)
                src2 = P.cc2ao if h_ < 2 else P.cc2bo
                d("sync", afm[i_],
                  src2[j_, 128 * (h_ % 2):128 * (h_ % 2) + 128, tks],
                  "sIN" if i_ < 16 else "sOUT")
            vload = P.ctrs["sOUT"].v
        else:
            for dt_ in range(NDT):
                d("sync", afm[dt_],
                  P.ln3sc[128 * dt_:128 * (dt_ + 1), tks], "sIN")
            vload = P.ctrs["sIN"].v
        w("tensor", "sIN", vload16 if vload16 is not None else vload)

        if not is5:
            for bb in (4, 5, 7):
                if bank.get(bb) is not None:
                    w("tensor", "sEV", bank[bb])
                    w("tensor", "sSC", bank_sc[bb])
            for tt in range(4):
                for dd in range(NDT):
                    sl = gwidx % 10
                    sem = f"sW{sl}"
                    if gwidx >= 10:
                        w("sync", "sMM", gtick[gwidx - 10])
                    d("sync", wslot[sl][:, 0:252],
                      P.glw[128 * dd:128 * (dd + 1), :], sem)
                    w("tensor", sem, P.ctrs[sem].v)
                    P.emit_raw("tensor", lambda e, p=P.ps[7][:, 0:28],
                               l=afm[dd][:, 128 * tt:128 * (tt + 1)],
                               r_=wslot[sl][:, 0:28],
                               st=(dd == 0), sp=(dd == NDT - 1):
                               P.nc.tensor.matmul(p, R(l), R(r_), start=st,
                                                  stop=sp))
                    P.emit_raw("tensor", lambda e, p=P.ps[4][:, 0:224],
                               l=afm[dd][:, 128 * tt:128 * (tt + 1)],
                               r_=wslot[sl][:, 28:252],
                               st=(dd == 0), sp=(dd == NDT - 1):
                               P.nc.tensor.matmul(p, R(l), R(r_), start=st,
                                                  stop=sp))
                    gtick[gwidx] = P.tick_last("tensor", "sMM")
                    gwidx += 1
                vg = gtick[gwidx - 1]
                w("vector", "sMM", vg)
                lg = P.ps[7][:, 0:28]
                m1 = STA[:, 96:97].bitcast(F32)
                m2 = STA[:, 97:98].bitcast(F32)
                p1 = STA[:, 98:99].bitcast(F32)
                p2 = STA[:, 99:100].bitcast(F32)
                dl = STA[:, 100:101].bitcast(F32)
                eq1 = ep[0][:, 0:28]; masked = ep[0][:, 32:60]
                eq2 = ep[0][:, 64:92]; gate = ep[0][:, 96:124]
                gexp = ep[1][:, 0:224]
                P.emit("vector", lambda e: e.reduce_max(m1, lg, axis=AX.X))
                P.emit("vector", lambda e: e.tensor_scalar(
                    eq1, lg, m1, None, ALU.is_ge))
                P.emit("vector", lambda e: e.scalar_tensor_tensor(
                    masked, eq1, -1e30, lg, ALU.mult, ALU.add))
                P.emit("vector", lambda e: e.reduce_max(m2, masked, axis=AX.X))
                vd = P.emit("vector", lambda e: e.tensor_sub(dl, m1, m2))
                P.emit("vector", lambda e: e.tensor_scalar(
                    eq2, masked, m2, None, ALU.is_ge))
                w("scalar", "sEV", vd)
                vsg = P.emit("scalar", lambda e: e.activation(
                    out=p1, in_=dl, func=AF.Sigmoid))
                w("vector", "sSC", vsg)
                P.emit("vector", lambda e: e.tensor_scalar_mul(p1, p1, 2.0))
                P.emit("vector", lambda e: e.tensor_scalar(
                    p2, p1, -1.0, 2.0, ALU.mult, ALU.add))
                P.emit("vector", lambda e: e.tensor_scalar_mul(gate, eq1, p1))
                P.emit("vector", lambda e: e.scalar_tensor_tensor(
                    gate, eq2, p2, gate, ALU.mult, ALU.add))
                ge3 = gexp.rearrange("p (e r) -> p e r", r=8)
                for r_ in range(8):
                    P.emit("vector", lambda e, r_=r_: e.tensor_copy(
                        ge3[:, :, r_], gate))
                vhs = P.emit("vector", lambda e: e.tensor_mul(
                    ep[2][:, 0:224], P.ps[4][:, 0:224], gexp))
                w("tensor", "sEV", vhs)
                P.emit_raw("tensor", lambda e: P.nc.tensor.transpose(
                    P.ps[5][:, 0:128], ep[2][:, 0:128].bitcast(F32),
                    P.sv_ident.bitcast(F32)))
                P.emit_raw("tensor", lambda e: P.nc.tensor.transpose(
                    P.ps[5][0:96, 128:256], ep[2][:, 128:224].bitcast(F32),
                    P.sv_ident.bitcast(F32)))
                vt1 = P.tick_last("tensor", "sMM")
                w("vector", "sMM", vt1)
                P.emit("vector", lambda e, tt=tt: e.tensor_copy(
                    hst[tt][:, 0:128], P.ps[5][:, 0:128]))
                vev = P.emit("vector", lambda e, tt=tt: e.tensor_copy(
                    hst[tt][0:96, 128:256], P.ps[5][0:96, 128:256]))
                hst_tick[tt] = vev

        if not is5:
            P._gate_done = P.ctrs["sEV"].v
            w("sync", "sMM", P.ctrs["sMM"].v)
        for pp in (8, 9):
            if P.ctrs[f"sS{pp}"].v:
                w("vector", f"sS{pp}", P.ctrs[f"sS{pp}"].v)
        for ec in range(8):
            pbank = 4 * (ec % 2)
            ntile = NDT + (0 if is5 else 2)
            for dd in range(ntile):
                gw = widx
                sl = gw % 10
                sem = f"sW{sl}"
                if gw >= 10:
                    w("sync", "sMM", wtick[gw - 10])
                if dd < NDT:
                    dt2 = (4 * P5SEQ[dd][0] + P5SEQ[dd][1]) if is5 else dd
                    blk = wsrc[128 * dt2:128 * (dt2 + 1),
                               512 * ec:512 * (ec + 1)]
                    d("sync", wslot[sl], blk, sem)
                elif dd == NDT:
                    d("sync", wslot[sl],
                      P.lbR[0:128, 512 * ec:512 * (ec + 1)], sem)
                else:
                    d("sync", wslot[sl][0:96, :],
                      P.lbR[128:224, 512 * ec:512 * (ec + 1)], sem)
                widx += 1
                w("tensor", sem, P.ctrs[sem].v)
                rhs = wslot[sl]
                if dd == 0 and ec == 1:
                    if not is5:
                        w("tensor", "sEV", P._gate_done)
                    elif tsup == 1:
                        for v_ in list(P._trev.values()):
                            w("tensor", "sEV", v_)
                if dd == 16 and ec == 0 and is5:
                    w("tensor", "sOUT", vload)
                if dd < NDT:
                    for tt in range(4):
                        if dd == 0 and bank.get(pbank + tt) is not None:
                            w("tensor", "sEV", bank[pbank + tt])
                            w("tensor", "sSC", bank_sc[pbank + tt])
                        P.emit_raw("tensor",
                                   lambda e, p=P.ps[pbank + tt][:, 0:512],
                                   l=afm[dd][:, 128 * tt:128 * (tt + 1)],
                                   r_=rhs, st=(dd == 0),
                                   sp=(is5 and dd == NDT - 1):
                                   P.nc.tensor.matmul(p, R(l), R(r_),
                                                      start=st, stop=sp))
                elif dd == NDT:
                    for tt in range(4):
                        w("tensor", "sEV", hst_tick[tt])
                        P.emit_raw("tensor",
                                   lambda e, p=P.ps[pbank + tt][:, 0:512],
                                   l=hst[tt][:, 0:128], r_=rhs:
                                   P.nc.tensor.matmul(p, R(l), R(r_),
                                                      start=False, stop=False))
                else:
                    for tt in range(4):
                        P.emit_raw("tensor",
                                   lambda e, p=P.ps[pbank + tt][:, 0:512],
                                   l=hst[tt][0:96, 128:256], r_=rhs[0:96, :]:
                                   P.nc.tensor.matmul(p, R(l), R(r_),
                                                      start=False, stop=True))
                wtick[gw] = P.tick_last("tensor", "sMM")
            vmm = wtick[widx - 1]
            w("scalar", "sMM", vmm)
            w("vector", "sMM", vmm)
            for tt in range(4):
                vsc = P.emit("scalar", lambda e,
                             o=stg[tt][:, 512 * ec:512 * (ec + 1)],
                             p=P.ps[pbank + tt][:, 0:512],
                             a=STA[:, 8 * tt + ec:8 * tt + ec + 1]:
                             e.activation(out=o, in_=p, func=AF.Copy,
                                          accum_out=a))
                w("vector", "sSC", vsc)
                vvc = P.emit("vector", lambda e, o=ep[2 + tt][:, 0:512],
                             p=stg[tt][:, 512 * ec:512 * (ec + 1)],
                             a=STA[:, 32 + 8 * tt + ec:32 + 8 * tt + ec + 1]:
                             e.scalar_tensor_tensor(
                                 o, p, 1.0, p, ALU.mult, ALU.mult,
                                 accum_out=a))
                bank[pbank + tt] = vvc
                bank_sc[pbank + tt] = vsc

        for tt in range(4):
            tglob = 512 * tsup + 128 * tt
            mu_ = STA[:, 64 + 8 * tt:65 + 8 * tt].bitcast(F32)
            rs_ = STA[:, 65 + 8 * tt:66 + 8 * tt].bitcast(F32)
            t1 = STA[:, 66 + 8 * tt:67 + 8 * tt].bitcast(F32)
            t2 = STA[:, 67 + 8 * tt:68 + 8 * tt].bitcast(F32)
            t3 = STA[:, 68 + 8 * tt:69 + 8 * tt].bitcast(F32)
            t4 = STA[:, 69 + 8 * tt:70 + 8 * tt].bitcast(F32)
            w("vector", "sSC", max(bank_sc[tt], bank_sc[4 + tt]))
            P.emit("vector", lambda e, o=t1, i=STA[:, 8 * tt:8 * tt + 8]:
                   e.reduce_sum(o, i, axis=AX.X))
            P.emit("vector", lambda e, o=mu_, i=t1:
                   e.tensor_scalar_mul(o, i, 1.0 / D))
            P.emit("vector", lambda e, o=t2,
                   i=STA[:, 32 + 8 * tt:32 + 8 * tt + 8]:
                   e.reduce_sum(o, i, axis=AX.X))
            P.emit("vector", lambda e, o=t2, i=t2:
                   e.tensor_scalar_mul(o, i, 1.0 / D))
            P.emit("vector", lambda e, o=t1, m=mu_: e.tensor_mul(o, m, m))
            P.emit("vector", lambda e, o=t2, a=t2, b=t1: e.tensor_sub(o, a, b))
            P.emit("vector", lambda e, o=t2: e.tensor_scalar_add(o, o, EPS))
            vrc = P.emit("vector", lambda e, o=t2: e.reciprocal(o, o))
            w("scalar", "sEV", vrc)
            vsq = P.emit("scalar", lambda e, o=rs_, i=t2:
                         e.activation(out=o, in_=i, func=AF.Sqrt))
            P.emit("vector", lambda e, o=mu_: e.tensor_scalar_mul(o, o, -1.0))
            w("vector", "sSC", vsq)
            xslot = P.XT[:, 0:4096] if tt % 2 == 0 else P.XT[:, 4096:8192]
            res_src = P.xtm if is5 else P.x2sc
            sem = "sXT0" if tt % 2 == 0 else "sXT1"
            if P._p56_xt.get(tt % 2) is not None:
                ssem, sval, vev_ = P._p56_xt[tt % 2]
                w("sync", ssem, sval)
                w("sync", "sEV", vev_)
            d("sync", xslot, res_src[tglob:tglob + 128, :], sem)
            w("vector", sem, P.ctrs[sem].v)
            P.emit("vector", lambda e, o=stg[tt], m=mu_, r_=rs_:
                   e.tensor_scalar(o, o, m, r_, ALU.add, ALU.mult))
            vap = P.emit("vector", lambda e, o=xslot, a=stg[tt], b=xslot:
                         e.tensor_add(o, a, b))
            w("gpsimd", "sEV", vap)
            if is5:
                xsem = f"sS{10 + tt % 2}"
                d("gpsimd", P.x2sc[tglob:tglob + 128, :], xslot, xsem)
                if "x2" in P.debug:
                    d("gpsimd", P.dbg["dx2"][tglob:tglob + 128, :], xslot,
                      xsem)
                _x2st = (xsem, P.ctrs[xsem].v)
                P.emit("vector", lambda e, o=t3, i=xslot:
                       e.reduce_sum(o, i, axis=AX.X))
                P.emit("vector", lambda e, o=t3: e.tensor_scalar_mul(
                    o, o, 1.0 / D))
                P.emit("vector", lambda e, o=junkof[tt], i=xslot, a=t4:
                       e.scalar_tensor_tensor(o, i, 1.0, i, ALU.mult,
                                              ALU.mult, accum_out=a))
                P.emit("vector", lambda e, o=t4: e.tensor_scalar_mul(
                    o, o, 1.0 / D))
                P.emit("vector", lambda e, o=t1, m=t3: e.tensor_mul(o, m, m))
                P.emit("vector", lambda e, o=t4, a=t4, b=t1:
                       e.tensor_sub(o, a, b))
                P.emit("vector", lambda e, o=t4: e.tensor_scalar_add(
                    o, o, EPS))
                vr2 = P.emit("vector", lambda e, o=t4: e.reciprocal(o, o))
                w("scalar", "sEV", vr2)
                vq2 = P.emit("scalar", lambda e, o=t1, i=t4:
                             e.activation(out=o, in_=i, func=AF.Sqrt))
                P.emit("vector", lambda e, o=t3: e.tensor_scalar_mul(
                    o, o, -1.0))
                w("vector", "sSC", vq2)
                vln3 = P.emit("vector", lambda e, o=stg[tt], i=xslot, m=t3,
                              r_=t1: e.tensor_scalar(o, i, m, r_, ALU.add,
                                                     ALU.mult))
                w("tensor", "sEV", vln3)
                if tt == 0:
                    for bb in (5, 6):
                        if bank.get(bb) is not None:
                            w("tensor", "sEV", bank[bb])
                            w("tensor", "sSC", bank_sc[bb])
                trev = P._trev
                trgp = P._trgp
                for blk in range(NDT):
                    pb = 5 if blk % 2 == 0 else 6
                    if trev.get(blk % 2) is not None:
                        w("tensor", "sEV", trev[blk % 2])
                    P.emit_raw("tensor", lambda e, p=P.ps[pb][:, 0:128],
                               i=stg[tt][:, 128 * blk:128 * (blk + 1)]:
                               P.nc.tensor.transpose(p, i.bitcast(F32),
                                                     P.sv_ident.bitcast(F32)))
                    vtr = P.tick_last("tensor", "sMM")
                    w("vector", "sMM", vtr)
                    if trgp.get(blk % 2) is not None:
                        w("vector", *trgp[blk % 2])
                    vte = P.emit("vector", lambda e,
                                 o=ep[2 + blk % 2][:, 0:128],
                                 p=P.ps[pb][:, 0:128]: e.tensor_copy(o, p))
                    trev[blk % 2] = vte
                    w("gpsimd", "sEV", vte)
                    tsem = f"sS{8 + blk % 2}"
                    d("gpsimd", P.ln3sc[128 * blk:128 * (blk + 1),
                                        tglob:tglob + 128],
                      ep[2 + blk % 2][:, 0:128], tsem)
                    trgp[blk % 2] = (tsem, P.ctrs[tsem].v)
                P._p56_xt[tt % 2] = (_x2st[0], _x2st[1], vln3)
            else:
                ysem = f"sS{10 + tt % 2}"
                d("gpsimd", P.y[tglob:tglob + 128, :], xslot, ysem)
                P._p56_xt[tt % 2] = (ysem, P.ctrs[ysem].v, vap)
    if is5:
        P.p5_store_total = {s: P.ctrs[s].v
                            for s in ("sS8", "sS9", "sS10", "sS11")}

def build(debug=None, stop_after=None):
    P = Prog(debug)
    P._p56_xt = {}
    phase_p0(P)
    phase_p1(P)
    if stop_after == "p1":
        for i in range(12):
            if P.ctrs[f"sS{i}"].v:
                P.wait("gpsimd", f"sS{i}", P.ctrs[f"sS{i}"].v)
        P.wait("gpsimd", "sST", P.ctrs["sST"].v)
        return P
    phase_p2(P)
    phase_p3(P)
    if stop_after == "p3":
        for i in range(12):
            if P.ctrs[f"sS{i}"].v:
                P.wait("gpsimd", f"sS{i}", P.ctrs[f"sS{i}"].v)
        return P
    phase_p4(P)
    phase_p56(P, 5)
    if stop_after == "p5":
        for i in range(12):
            if P.ctrs[f"sS{i}"].v:
                P.wait("gpsimd", f"sS{i}", P.ctrs[f"sS{i}"].v)
        return P
    phase_p56(P, 6)
    P.wait("gpsimd", "sS10", P.ctrs["sS10"].v)
    P.wait("gpsimd", "sS11", P.ctrs["sS11"].v)
    return P

def kernel(**inputs):
    per_core = stage_inputs(inputs)
    P = build()
    nc = P.finalize()
    res = run_bass_kernel_spmd(nc, per_core, core_ids=list(range(NCORE)))
    out = np.concatenate([res.results[c]["y"] for c in range(NCORE)], axis=0)
    return out.reshape(B, S, D)


# version 14 (final)
# speedup vs baseline: 1.0128x; 1.0128x over previous
import math
import numpy as np
from contextlib import ExitStack

import concourse.bass as bass
import concourse.mybir as mybir
from concourse.bass_utils import run_bass_kernel_spmd
from concourse import aot_env

F32 = mybir.dt.float32
F32R = mybir.dt.float32r
AF = mybir.ActivationFunctionType
ALU = mybir.AluOpType
AX = mybir.AxisListType

D = 4096; H = 32; DH = 128; S = 4096; B = 2; NT = 8192
NCORE = 8; TPC = NT // NCORE
CLIP = 8.0; EPS = 1e-5; SCALE = DH ** -0.5
NDT = D // 128
QK_SLAB = 1024 * TPC
V_SLAB = TPC * 512
SLAB = QK_SLAB + V_SLAB

SLOPES = (1.0 / 2.0 ** (np.arange(1, H + 1, dtype=np.float64) * (8.0 / H)))
SLOT_CW = [64, 256, 512, 512]
SLOT_WIN = [768, 2560, 4096, 4096]
HEADS_OF = [[c, c + 8, c + 16, c + 24] for c in range(NCORE)]

ENGINES = ("sync", "scalar", "vector", "tensor", "gpsimd")

class Ctr:
    __slots__ = ("v",)
    def __init__(self): self.v = 0
    def add(self, n=1):
        self.v += n
        return self.v

def stage_inputs(ins):
    x2d = np.ascontiguousarray(
        np.asarray(ins["L0_attn_hidden_states"], np.float32).reshape(NT, D))
    Wqkv = np.asarray(ins["L0_attn_Wqkv"], np.float32)
    wqT = np.ascontiguousarray(Wqkv[0:D].T)
    wkT = np.ascontiguousarray(Wqkv[D:2 * D].T)
    vperm = np.concatenate([np.arange(128 * h, 128 * h + 128)
                            for j in range(8) for h in HEADS_OF[j]])
    wvTp = np.ascontiguousarray(Wqkv[2 * D:3 * D].T[:, vperm])
    woutTp = np.ascontiguousarray(
        np.asarray(ins["L0_attn_Wout"], np.float32).T[vperm])
    bwT = np.ascontiguousarray(np.asarray(ins["L0_ffn_base_W"], np.float32).T)
    gl = np.concatenate([
        np.asarray(ins["L0_ffn_gate_W"], np.float32).T,
        np.asarray(ins["L0_ffn_lora_A"], np.float32).reshape(224, D).T],
        axis=1)
    glw = np.ascontiguousarray(gl)
    lbR = np.ascontiguousarray(
        np.asarray(ins["L0_ffn_lora_B"], np.float32).transpose(0, 2, 1)
        .reshape(224, D))
    lnc = {k: np.asarray(ins[v], np.float32).reshape(H, 128)
           for k, v in (("qw", "L0_attn_q_ln_w"), ("qb", "L0_attn_q_ln_b"),
                        ("kw", "L0_attn_k_ln_w"), ("kb", "L0_attn_k_ln_b"))}
    i_ = np.arange(128.0)[:, None]
    j_ = np.arange(67.0)[None, :]
    alib_all = (SLOPES[:, None, None] *
                (i_[None] - (63.0 + 64.0 * j_[None]))).astype(np.float32)
    iK = np.arange(128)[:, None]
    qL = np.arange(512)[None, :]
    masks = np.stack([(iK <= qL - (64 * idx - 64)).astype(np.float32)
                      for idx in range(9)])
    ones = np.ones((128, 128), np.float32)
    ident = np.eye(128, dtype=np.float32)
    cmask = np.where(np.arange(128)[None, :] <= np.arange(128)[:, None],
                     0.0, -3.0e38).astype(np.float32)
    per_core = []
    for c in range(NCORE):
        T = slice(TPC * c, TPC * (c + 1))
        hs = HEADS_OF[c]
        alib = np.ascontiguousarray(
            alib_all[hs].transpose(1, 0, 2).reshape(128, 4 * 67))
        lncols = np.stack([v for hl in range(4) for v in
                           (lnc["qw"][hs[hl]], lnc["qb"][hs[hl]],
                            lnc["kw"][hs[hl]], lnc["kb"][hs[hl]])],
                          axis=1)
        h0 = hs[0]
        alib2d = np.tile((SLOPES[h0] * (np.arange(896.0) - 895.0))
                         .astype(np.float32).reshape(1, 896), (128, 1))
        alib2d[:, 768:896] += cmask
        per_core.append(dict(
            alib2d=alib2d,
            xfm=np.ascontiguousarray(x2d[T].T),
            xtm=np.ascontiguousarray(x2d[T]),
            wqT=wqT, wkT=wkT, wvTp=wvTp, woutTp=woutTp, bwT=bwT,
            glw=glw, lbR=lbR, alib=alib,
            lncols=np.ascontiguousarray(lncols),
            masks=np.ascontiguousarray(masks.reshape(9 * 128, 512)),
            ones=ones, ident=ident))
    return per_core

class Prog:
    def __init__(self, debug=None):
        self.debug = debug or ()
        nc = self.nc = bass.Bass("TRN2", target_bir_lowering=False, debug=False)
        self.ops = {e: [] for e in ENGINES}
        st = self.st = ExitStack()
        g = lambda n, s, k="ExternalInput", **kw: nc.dram_tensor(
            n, s, F32R, kind=k, **kw).ap()
        self.xfm = g("xfm", [D, TPC]); self.xtm = g("xtm", [TPC, D])
        self.wqT = g("wqT", [D, D]); self.wkT = g("wkT", [D, D])
        self.wvTp = g("wvTp", [D, D]); self.woutTp = g("woutTp", [D, D])
        self.bwT = g("bwT", [D, D]); self.glw = g("glw", [D, 252])
        self.lbR = g("lbR", [224, D])
        self.lncolsd = g("lncols", [128, 16])
        self.alibd = g("alib", [128, 4 * 67])
        self.masksd = g("masks", [9 * 128, 512])
        self.onesd = g("ones", [128, 128]); self.identd = g("ident", [128, 128])
        self.alib2dd = g("alib2d", [128, 896])
        self.y = g("y", [TPC, D], k="ExternalOutput")
        self.cc1q_in = g("cc1q_in", [8, QK_SLAB // 2], k="Internal")
        self.cc1q_out = g("cc1q_out", [8, QK_SLAB // 2], k="Internal",
                          addr_space="Local")
        self.cc1k_in = g("cc1k_in", [8, QK_SLAB // 2], k="Internal")
        self.cc1k_out = g("cc1k_out", [8, QK_SLAB // 2], k="Internal",
                          addr_space="Local")
        self.cc1v_in = g("cc1v_in", [8, V_SLAB], k="Internal")
        self.cc1v_out = g("cc1v_out", [8, V_SLAB], k="Internal",
                          addr_space="Local")
        self.ccs_in = g("ccs_in", [4, TPC], k="Internal")
        self.ccs_out = g("ccs_out", [32, TPC], k="Internal", addr_space="Shared")
        self.cc2a_in = g("cc2a_in", [8, 256 * TPC], k="Internal")
        self.cc2a_out = g("cc2a_out", [8, 256 * TPC], k="Internal",
                          addr_space="Local")
        self.cc2b_in = g("cc2b_in", [8, 256 * TPC], k="Internal")
        self.cc2b_out = g("cc2b_out", [8, 256 * TPC], k="Internal",
                          addr_space="Local")
        self.x2sc = g("x2sc", [TPC, D], k="Internal")
        self.ln3sc = g("ln3sc", [D, TPC], k="Internal")
        self.dbg = {}
        if "qkv" in self.debug:
            self.dbg["dqk"] = g("dqk", [8 * 1024, TPC], k="ExternalOutput")
            self.dbg["dv"] = g("dv", [8 * TPC, 512], k="ExternalOutput")
            self.dbg["dst"] = g("dst", [4, TPC], k="ExternalOutput")
        if "attn" in self.debug:
            self.dbg["dao"] = g("dao", [8 * 512, TPC], k="ExternalOutput")
        if "x2" in self.debug:
            self.dbg["dx2"] = g("dx2", [TPC, D], k="ExternalOutput")

        self.cc1qi = self.cc1q_in.rearrange("s (f t) -> s f t", t=TPC)
        self.cc1ki = self.cc1k_in.rearrange("s (f t) -> s f t", t=TPC)
        self.cc1qo = self.cc1q_out.rearrange("s (f t) -> s f t", t=TPC)
        self.cc1ko = self.cc1k_out.rearrange("s (f t) -> s f t", t=TPC)
        self.cc1i_v = self.cc1v_in.rearrange("s (t f) -> s t f", f=512)
        self.cc1o_v = self.cc1v_out.rearrange("s (t f) -> s t f", f=512)
        self.cc2ai = self.cc2a_in.rearrange("s (f t) -> s f t", t=TPC)
        self.cc2ao = self.cc2a_out.rearrange("s (f t) -> s f t", t=TPC)
        self.cc2bi = self.cc2b_in.rearrange("s (f t) -> s f t", t=TPC)
        self.cc2bo = self.cc2b_out.rearrange("s (f t) -> s f t", t=TPC)
        self.ccso = self.ccs_out.rearrange("(s r) t -> s r t", r=4)

        sbuf = lambda n, p, f: st.enter_context(
            nc.sbuf_tensor(n, [p, f], F32R))
        self.AR = sbuf("AR", 128, 32768)
        self.EP = sbuf("EP", 128, 4096)
        self.WRG = sbuf("WRG", 128, 6144)
        self.SV = sbuf("SV", 128, 1664)
        self.STA = st.enter_context(nc.sbuf_tensor("STA", [128, 192], F32))
        self.XT = sbuf("XT", 128, 8192)
        self.ps = [st.enter_context(nc.psum_tensor(f"ps{i}", [128, 512], F32))
                   for i in range(8)]
        self.sems = {}
        self.ctrs = {}
        for name in ("sIN", "sMM", "sEV", "sSC", "sGP", "sCC", "sOUT", "sGE",
                     "sXT0", "sXT1", "sST", *[f"sW{i}" for i in range(10)],
                     *[f"sS{i}" for i in range(12)]):
            self.sems[name] = st.enter_context(nc.semaphore(name))
            self.ctrs[name] = Ctr()

        sv = self.SV
        self.sv_lnc = sv[:, 0:16]
        self.sv_alib = sv[:, 16:284]
        self.sv_ones = sv[:, 284:412]
        self.sv_ident = sv[:, 412:540]
        self.row_mu1 = sv[0:1, 540:1564]
        self.row_rs1 = sv[32:33, 540:1564]
        self.row_stmu = sv[0:1, 540:1564]
        self.row_strs = sv[64:65, 540:1564]
        self.row_tmp = sv[32:33, 540:1052]
        self.row_tmp2 = sv[32:33, 1052:1564]
        self.row_li = [sv[64:65, 540:1052], sv[64:65, 1052:1564]]
        self.row_ap = [sv[0:1, 540 + 512 * i:1052 + 512 * i]
                       for i in range(2)] \
            + [sv[32:33, 540 + 512 * i:1052 + 512 * i] for i in range(2)]
        self.col_small = self.STA[:, 0:192]
        self.sv_masks = self.XT[:, 2048:6656]

    CHAIN = {"vector": "sEV", "scalar": "sSC", "gpsimd": "sGE"}

    def emit(self, eng, fn):
        ch = self.CHAIN.get(eng)
        if ch is None:
            self.ops[eng].append(fn)
            return None
        sem = self.sems[ch]
        idx = self.ctrs[ch].add(1)
        def wrapped(e, fn=fn, idx=idx, sem=sem):
            if idx > 1:
                e.wait_ge(sem, idx - 1)
            fn(e).then_inc(sem, 1)
        self.ops[eng].append(wrapped)
        return idx

    def emit_raw(self, eng, fn):
        self.ops[eng].append(fn)

    def dma(self, eng, out, in_, sem="sIN"):
        s = self.sems[sem]
        v = self.ctrs[sem].add(16)
        self.ops[eng].append(lambda e, o=out, i=in_, s_=s: e.dma_start(
            out=o, in_=i).then_inc(s_, 16))
        return v

    def wait(self, eng, sem, v):
        self.ops[eng].append(
            lambda e, s=self.sems[sem], vv=v: e.wait_ge(s, vv))

    def tick(self, eng, sem):
        s = self.sems[sem]
        v = self.ctrs[sem].add(1)
        return v, (lambda inst, s_=s: inst.then_inc(s_, 1))

    def tick_last(self, eng, sem):
        v, inc = self.tick(eng, sem)
        f = self.ops[eng].pop()
        self.ops[eng].append(lambda e, f=f, i=inc: i(f(e)))
        return v

    def R(self, ap):
        return ap.bitcast(F32R)

    def finalize(self):
        nc = self.nc
        with nc.allow_low_precision(
                "f32r storage everywhere; sums accumulate fp32 in-engine "
                "and round to f32r (2^-13) on store"), nc.Block() as block:
            for eng in ENGINES:
                def runner(e, ops=self.ops[eng]):
                    for f in ops:
                        f(e)
                getattr(block, eng)(runner)
        return nc

def phase_p0(P):
    d, w = P.dma, P.wait
    R = P.R
    d("sync", P.sv_lnc, P.lncolsd[:])
    d("sync", P.sv_alib, P.alibd[:])
    d("sync", P.sv_ones, P.onesd[:])
    d("sync", P.sv_ident, P.identd[:])
    P.xa = [P.AR[:, 1024 * i:1024 * (i + 1)] for i in range(NDT)]
    for i in range(NDT):
        d("sync", P.xa[i], P.xfm[128 * i:128 * (i + 1), :])
    vconsts = P.ctrs["sIN"].v

    xt = [P.XT[:, 0:4096], P.XT[:, 4096:8192]]
    sums = P.col_small[:, 0:8]
    sqs = P.col_small[:, 8:16]
    P.ev_xtm = []
    for j in range(8):
        slot = j % 2
        sem = "sXT0" if slot == 0 else "sXT1"
        if j >= 2:
            w("sync", "sEV", P.ev_xtm[j - 2])
        d("sync", xt[slot], P.xtm[128 * j:128 * (j + 1), :], sem)
        w("vector", sem, P.ctrs[sem].v)
        P.emit("vector", lambda e, o=sums[:, j:j + 1], i=xt[slot]:
               e.reduce_sum(o, i, axis=AX.X))
        v = P.emit("vector", lambda e, o=xt[slot], i=xt[slot],
                   a=sqs[:, j:j + 1]: e.scalar_tensor_tensor(
                       o, i, 1.0, i, ALU.mult, ALU.mult, accum_out=a))
        P.ev_xtm.append(v)
    mu8 = P.col_small[:, 16:24]; rs8 = P.col_small[:, 24:32]
    var8 = P.col_small[:, 32:40]
    P.emit("vector", lambda e: e.tensor_scalar_mul(mu8, sums, 1.0 / D))
    P.emit("vector", lambda e: e.scalar_tensor_tensor(
        var8, mu8, 1.0, mu8, ALU.mult, ALU.mult))
    P.emit("vector", lambda e: e.scalar_tensor_tensor(
        var8, sqs, 1.0 / D, var8, ALU.mult, ALU.subtract))
    P.emit("vector", lambda e: e.tensor_scalar_add(var8, var8, EPS))
    vv = P.emit("vector", lambda e: e.reciprocal(var8, var8))
    w("scalar", "sEV", vv)
    vs = P.emit("scalar", lambda e: e.activation(
        out=rs8, in_=var8, func=AF.Sqrt))
    w("tensor", "sIN", vconsts)
    w("tensor", "sSC", vs)
    P.emit_raw("tensor", lambda e: P.nc.tensor.transpose(
        P.ps[6][0:16, 0:128], P.col_small[:, 16:32].bitcast(F32),
        P.sv_ident.bitcast(F32)))
    vtr = P.tick_last("tensor", "sMM")
    w("vector", "sMM", vtr)
    ve = P.emit("vector", lambda e: e.tensor_copy(
        P.EP[0:16, 0:128], P.ps[6][0:16, 0:128]))
    w("gpsimd", "sEV", ve)
    for j in range(8):
        d("gpsimd", P.row_mu1[0:1, 128 * j:128 * (j + 1)],
          P.EP[j:j + 1, 0:128], "sGP")
        d("gpsimd", P.row_rs1[0:1, 128 * j:128 * (j + 1)],
          P.EP[8 + j:9 + j, 0:128], "sGP")
    vrows = P.ctrs["sGP"].v
    w("tensor", "sGP", vrows)
    for th in range(2):
        P.emit_raw("tensor", lambda e, th=th: P.nc.tensor.matmul(
            P.ps[0 + th][:, 0:512], R(P.sv_ones[0:1, 0:128]),
            R(P.row_mu1[0:1, 512 * th:512 * (th + 1)]), start=True, stop=True))
        P.emit_raw("tensor", lambda e, th=th: P.nc.tensor.matmul(
            P.ps[2 + th][:, 0:512], R(P.sv_ones[32:33, 0:128]),
            R(P.row_rs1[0:1, 512 * th:512 * (th + 1)]), start=True, stop=True))
    vbc = P.tick_last("tensor", "sMM")
    w("vector", "sMM", vbc)
    ep = [P.EP[:, 512 * i:512 * (i + 1)] for i in range(8)]
    last = None
    for i in range(NDT):
        for th in range(2):
            xi = P.xa[i][:, 512 * th:512 * (th + 1)]
            s = ep[4 + (i % 2) * 2 + th]
            P.emit("vector", lambda e, o=s, x_=xi, m=P.ps[th][:, 0:512]:
                   e.scalar_tensor_tensor(o, x_, 1.0, m, ALU.mult,
                                          ALU.subtract))
            last = P.emit("vector", lambda e, o=xi, a=s,
                          r_=P.ps[2 + th][:, 0:512]: e.tensor_mul(o, a, r_))
    P.vx_norm = last

def phase_p1(P):
    d, w = P.dma, P.wait
    R = P.R
    ep = [P.EP[:, 512 * i:512 * (i + 1)] for i in range(8)]
    wsl = [P.WRG[:, 2048 * i:2048 * (i + 1)] for i in range(3)]
    w("tensor", "sEV", P.vx_norm)

    vmm_tick = {}
    vevac = {}
    vstore = {}
    for vo in range(8):
        for q4 in range(8):
            gidx = vo * 8 + q4
            slot = gidx % 3
            sem = f"sW{slot}"
            if gidx >= 3:
                w("sync", "sMM", vmm_tick[gidx - 3])
            blk = P.wvTp[512 * q4:512 * (q4 + 1), 512 * vo:512 * (vo + 1)]
            d("sync", wsl[slot].rearrange("p (i f) -> p i f", i=4),
              blk.rearrange("(i p) f -> p i f", p=128), sem)
            w("tensor", sem, P.ctrs[sem].v)
            for dd in range(4 * q4, 4 * (q4 + 1)):
                rhs = wsl[slot][:, 512 * (dd - 4 * q4):
                                512 * (dd - 4 * q4) + 512]
                for tt in range(8):
                    if dd == 0 and vo >= 1:
                        w("tensor", "sEV", vevac[(vo - 1, tt)])
                    lhs = P.xa[dd][:, 128 * tt:128 * (tt + 1)]
                    P.emit_raw("tensor", lambda e, p=P.ps[tt][:, 0:512],
                               l=lhs, r_=rhs, st=(dd == 0),
                               sp=(dd == NDT - 1):
                               P.nc.tensor.matmul(p, R(l), R(r_),
                                                  start=st, stop=sp))
            vmm_tick[gidx] = P.tick_last("tensor", "sMM")
        w("vector", "sMM", vmm_tick[vo * 8 + 7])
        for tt in range(8):
            ssem = f"sS{tt}"
            if vo >= 1:
                w("vector", ssem, vstore[(vo - 1, tt)])
            s = ep[tt]
            vevac[(vo, tt)] = P.emit("vector", lambda e, o=s,
                                     p=P.ps[tt][:, 0:512]:
                                     e.tensor_scalar(o, p, CLIP, -CLIP,
                                                     ALU.min, ALU.max))
            w("gpsimd", "sEV", vevac[(vo, tt)])
            d("gpsimd", P.cc1i_v[vo, 128 * tt:128 * (tt + 1), :], s, ssem)
            if "qkv" in P.debug:
                dvv = P.dbg["dv"].rearrange("(s t) f -> s t f", t=TPC)
                d("gpsimd", dvv[vo, 128 * tt:128 * (tt + 1), :], s, ssem)
            vstore[(vo, tt)] = P.ctrs[ssem].v
    vcc, vinc = P.tick("gpsimd", "sCC")
    for i in range(8):
        w("gpsimd", f"sS{i}", P.ctrs[f"sS{i}"].v)
    P.emit_raw("gpsimd", lambda e, inc=vinc: inc(_cc(e, "AllToAll",
        P.cc1v_in[:], P.cc1v_out[:])))
    P.cc1v_done = vcc

    w("tensor", "sEV", vevac[(7, 7)])
    for i in range(8):
        w("vector", f"sS{i}", P.ctrs[f"sS{i}"].v)
    evac_tick = {}
    mm_half = {}
    sq_tick = {}
    store_total = {}
    for ft in range(64):
        isq = ft < 32
        src_ = P.wqT if isq else P.wkT
        fcol = 128 * (ft % 32)
        b0 = 2 * (ft % 2)
        for half in range(2):
            gidx = 2 * ft + half
            slot = gidx % 5
            wsl5 = wsl + [P.XT[:, 0:2048], P.XT[:, 2048:4096]]
            sem = ("sW0", "sW1", "sW2", "sW5", "sW9")[slot]
            if gidx >= 5:
                w("sync", "sMM", mm_half[gidx - 5])
            elif slot < 3:
                w("sync", "sMM", vmm_tick[{0: 63, 1: 61, 2: 62}[slot]])
            blk = src_[2048 * half:2048 * (half + 1), fcol:fcol + 128]
            d("sync", wsl5[slot].rearrange("p (i f) -> p i f", i=16),
              blk.rearrange("(i p) f -> p i f", p=128), sem)
            w("tensor", sem, P.ctrs[sem].v)
            if half == 0 and ft >= 2:
                w("tensor", "sEV", evac_tick[ft - 2])
                w("tensor", "sMM", sq_tick[ft - 2])
            for dd in range(16 * half, 16 * (half + 1)):
                lhs = wsl5[slot][:, 128 * (dd - 16 * half):
                                 128 * (dd - 16 * half) + 128]
                for th in range(2):
                    P.emit_raw("tensor", lambda e, p=P.ps[b0 + th][:, 0:512],
                               l=lhs, r_=P.xa[dd][:, 512 * th:512 * (th + 1)],
                               st=(dd == 0), sp=(dd == NDT - 1):
                               P.nc.tensor.matmul(p, R(l), R(r_),
                                                  start=st, stop=sp))
            mm_half[gidx] = P.tick_last("tensor", "sMM")
        w("vector", "sMM", mm_half[2 * ft + 1])
        if ft >= 2:
            for ssem, sval in store_total[ft - 2].items():
                w("vector", ssem, sval)
            w("vector", "sSC", P.ctrs["sSC"].v)
        eA = [ep[0 + (ft % 2)], ep[2 + (ft % 2)]]
        eB = [ep[4 + (ft % 2)], ep[6 + (ft % 2)]]
        for th in range(2):
            evac_tick[ft] = P.emit("vector", lambda e, o=eA[th],
                                   p=P.ps[b0 + th][:, 0:512]:
                                   e.tensor_scalar(o, p, CLIP, -CLIP,
                                                   ALU.min, ALU.max))
        w("scalar", "sEV", evac_tick[ft])
        if ft >= 2:
            w("scalar", "sMM", sq_tick[ft - 2])
        vsq = None
        for th in range(2):
            vsq = P.emit("scalar", lambda e, o=eB[th], i=eA[th]:
                         e.activation(out=o, in_=i, func=AF.Square))
        w("tensor", "sSC", vsq)
        first = ft in (0, 32)
        last = ft in (31, 63)
        for th in range(2):
            P.emit_raw("tensor", lambda e, p=P.ps[4 + 2 * th][0:1, 0:512],
                       r_=eA[th], st=first, sp=last: P.nc.tensor.matmul(
                           p, R(P.sv_ones[:, 0:1]), R(r_), start=st, stop=sp))
            P.emit_raw("tensor", lambda e, p=P.ps[5 + 2 * th][0:1, 0:512],
                       r_=eB[th], st=first, sp=last: P.nc.tensor.matmul(
                           p, R(P.sv_ones[:, 0:1]), R(r_), start=st, stop=sp))
        sq_tick[ft] = P.tick_last("tensor", "sMM")
        w("gpsimd", "sEV", evac_tick[ft])
        h = ft if isq else ft - 32
        r0 = 128 * (h // 8)
        tgt = P.cc1qi if isq else P.cc1ki
        sts = {}
        for th in range(2):
            ssem = f"sS{(ft % 2) * 2 + th}"
            d("gpsimd", tgt[h % 8, r0:r0 + 128,
                            512 * th:512 * (th + 1)], eA[th], ssem)
            if "qkv" in P.debug:
                dv = P.dbg["dqk"].rearrange("(s f) t -> s f t", f=1024)
                d("gpsimd", dv[h % 8, (0 if isq else 512) + r0:
                               (0 if isq else 512) + r0 + 128,
                               512 * th:512 * (th + 1)], eA[th], ssem)
            sts[ssem] = P.ctrs[ssem].v
        store_total[ft] = sts
        if last:
            w("vector", "sMM", sq_tick[ft])
            w("vector", "sST", P.ctrs["sST"].v)
            base = 0 if isq else 2
            vs2 = None
            for th in range(2):
                pS = P.ps[4 + 2 * th][0:1, 0:512]
                pQ = P.ps[5 + 2 * th][0:1, 0:512]
                murow = P.row_stmu[0:1, 512 * th:512 * (th + 1)]
                rsrow = P.row_strs[0:1, 512 * th:512 * (th + 1)]
                tmp = P.row_tmp
                tmp2 = P.row_tmp2
                if th == 1:
                    w("vector", "sSC", vs2)
                P.emit("vector", lambda e, o=murow, i=pS:
                       e.tensor_scalar_mul(o, i, 1.0 / D))
                P.emit("vector", lambda e, o=tmp, i=pQ:
                       e.tensor_scalar_mul(o, i, 1.0 / D))
                P.emit("vector", lambda e, o=tmp2, m=murow:
                       e.tensor_mul(o, m, m))
                P.emit("vector", lambda e, o=tmp, a=tmp, b=tmp2:
                       e.tensor_sub(o, a, b))
                P.emit("vector", lambda e, o=tmp:
                       e.tensor_scalar_add(o, o, EPS))
                vv = P.emit("vector", lambda e, o=tmp: e.reciprocal(o, o))
                w("scalar", "sEV", vv)
                vs2 = P.emit("scalar", lambda e, o=rsrow, i=tmp:
                             e.activation(out=o, in_=i, func=AF.Sqrt))
            w("tensor", "sSC", vs2)
            w("gpsimd", "sSC", vs2)
            d("gpsimd", P.ccs_in[base:base + 1, :], P.row_stmu, "sST")
            d("gpsimd", P.ccs_in[base + 1:base + 2, :], P.row_strs, "sST")
            if "qkv" in P.debug:
                d("gpsimd", P.dbg["dst"][base:base + 1, :], P.row_stmu, "sST")
                d("gpsimd", P.dbg["dst"][base + 1:base + 2, :], P.row_strs,
                  "sST")
        if ft == 31:
            vq, incq = P.tick("gpsimd", "sCC")
            for i in range(4):
                w("gpsimd", f"sS{i}", P.ctrs[f"sS{i}"].v)
            P.emit_raw("gpsimd", lambda e, inc=incq: inc(_cc(e, "AllToAll",
                P.cc1q_in[:], P.cc1q_out[:])))
            P.cc1q_done = vq

    v2, inc2 = P.tick("gpsimd", "sCC")
    w("gpsimd", "sST", P.ctrs["sST"].v)
    P.emit_raw("gpsimd", lambda e, inc=inc2: inc(_cc(e, "AllGather",
        P.ccs_in[:], P.ccs_out[:])))
    P.ccs_done = v2
    v3, inc3 = P.tick("gpsimd", "sCC")
    for i in range(4):
        w("gpsimd", f"sS{i}", P.ctrs[f"sS{i}"].v)
    P.emit_raw("gpsimd", lambda e, inc=inc3: inc(_cc(e, "AllToAll",
        P.cc1k_in[:], P.cc1k_out[:])))
    P.cc1k_done = v3

def _cc(e, kind, in_, out):
    with aot_env.allow_ici_all_to_all():
        return e.collective_compute(
            kind, ALU.bypass, ins=[in_], outs=[out],
            replica_groups=[[0, 1, 2, 3, 4, 5, 6, 7]])

def phase_p2(P):
    d, w, t = P.dma, P.wait, P.tick
    w("sync", "sEV", P.ctrs["sEV"].v)
    w("sync", "sMM", P.ctrs["sMM"].v)
    d("sync", P.sv_masks.rearrange("p (m q) -> p m q", m=9),
      P.masksd.rearrange("(m p) q -> p m q", p=128), "sW5")
    P.masks_done = ("sW5", P.ctrs["sW5"].v)



def _emit_slot0_job(P, job_i, b, s, arK, arQ, arV, pw_at):
    d, w = P.dma, P.wait
    R = P.R
    win = SLOT_WIN[0]
    if P._norm_pend is not None:
        _emit_norm(P, arK, arQ)
    if P._st_now.get(0) is not None:
        w("vector", "sS10", P._st_now[0])
    if P._st_now.get(1) is not None:
        w("vector", "sS11", P._st_now[1])
    w("tensor", "sEV", P.ctrs["sEV"].v)
    w("tensor", "sSC", P.ctrs["sSC"].v)
    w("tensor", *P.p3_rows_done)
    w("vector", *P.p3_rows_done)
    msl = [P.EP[:, 0:896], P.EP[:, 1024:1920]]
    ptt = [P.EP[:, 2048:2944], P.EP[:, 3072:3968]]
    osl = [P.XT[:, 1024:1152], P.XT[:, 1152 + 128:1152 + 256]]
    negm = [P.col_small[:, 106:107], P.col_small[:, 107:108]]
    lcol = [P.col_small[:, 108:109], P.col_small[:, 109:110]]
    ab = P.XT[:, 6656:7552]
    for QT in range(32):
        for fn in pw_at.get(QT, []):
            fn()
        par = QT % 2
        q0 = 128 * QT
        kl = max(0, (q0 - win + 1)) // 128
        kh = QT
        nk = kh - kl + 1
        width = 128 * nk
        if P._s0_scores is not None:
            w("tensor", "sEV", P._s0_scores)
        idx0 = 128 * (kl - QT) + 768
        nch = (width + 511) // 512
        for cchunk in range(nch):
            c0 = 512 * cchunk
            cN = min(512, width - c0)
            P.emit_raw("tensor", lambda e, p=P.ps[cchunk][:, 0:cN], c0=c0,
                       l=arQ[s][:, q0:q0 + 128],
                       r_=arK[s][:, 128 * kl + c0:128 * kl + c0 + cN]:
                       P.nc.tensor.matmul(p, R(l), R(r_),
                                          start=True, stop=True))
        vqk = P.tick_last("tensor", "sMM")
        if P._s0_tpend is not None:
            P._s0_tpend()
            P._s0_tpend = None
        w("vector", "sMM", vqk)
        if P._s0_msc.get(par) is not None:
            w("vector", "sSC", P._s0_msc[par])
        if P._s0_ptt.get(par) is not None:
            w("vector", "sMM", P._s0_ptt[par])
        vlast = None
        for cb in range(nch):
            lo = 512 * cb
            hi = min(lo + 512, width)
            vlast = P.emit("vector", lambda e, o=msl[par][:, lo:hi],
                           p=P.ps[cb][:, 0:hi - lo],
                           m=ab[:, idx0 + lo:idx0 + hi]:
                           e.scalar_tensor_tensor(o, p, SCALE, m,
                                                  ALU.mult, ALU.add))
        P._s0_scores = vlast
        P.emit("vector", lambda e, o=negm[par], i=msl[par][:, 0:width]:
               e.reduce_max(o, i, axis=AX.X))
        vng = P.emit("vector", lambda e, o=negm[par]:
                     e.tensor_scalar_mul(o, o, -1.0))
        w("scalar", "sEV", vng)
        vex = P.emit("scalar", lambda e, o=msl[par][:, 0:width],
                     i=msl[par][:, 0:width], bias=negm[par], a=lcol[par]:
                     e.activation(out=o, in_=i, func=AF.Exp, bias=bias,
                                  accum_out=a))
        w("vector", "sSC", vex)
        vrc = P.emit("vector", lambda e, o=lcol[par]: e.reciprocal(o, o))
        if P._s0_vpend is not None:
            P._s0_vpend()
            P._s0_vpend = None
        w("scalar", "sEV", vrc)
        vpn = P.emit("scalar", lambda e, o=msl[par][:, 0:width],
                     i=msl[par][:, 0:width], m=lcol[par]:
                     e.activation(out=o, in_=i, func=AF.Identity, scale=m))
        P._s0_msc[par] = vpn

        def mkT(par=par, nk=nk, kl=kl, s=s, q0=q0, b=b, vpn=vpn):
            pendPV = None
            for blk in range(nk):
                tb = 2 + blk % 2
                if P._s0_trc.get(blk % 2) is not None:
                    w("tensor", "sSC", P._s0_trc[blk % 2])
                if blk == 0:
                    w("tensor", "sSC", vpn)
                P.emit_raw("tensor", lambda e, p=P.ps[tb][:, 0:128],
                           i=msl[par][:, 128 * blk:128 * (blk + 1)]:
                           P.nc.tensor.transpose(p, i.bitcast(F32),
                                                 P.sv_ident.bitcast(F32)))
                vtr = P.tick_last("tensor", "sMM")
                w("scalar", "sMM", vtr)
                vtc = P.emit("scalar", lambda e, o=ptt[par][:, 128 * blk:
                             128 * (blk + 1)], p=P.ps[tb][:, 0:128]:
                             e.activation(out=o, in_=p, func=AF.Copy))
                P._s0_trc[blk % 2] = vtc
                if pendPV is not None:
                    pendPV()
                def mkPV(blk=blk, vtc=vtc, st=(blk == 0),
                         sp=(blk == nk - 1)):
                    if st and P._s0_pv.get(par) is not None:
                        w("tensor", "sEV", P._s0_pv[par])
                    w("tensor", "sSC", vtc)
                    P.emit_raw("tensor", lambda e,
                               p=P.ps[6 + par][:, 0:128],
                               l=arV[s][:, 128 * (kl + blk):
                                        128 * (kl + blk + 1)],
                               r_=ptt[par][:, 128 * blk:128 * (blk + 1)],
                               st=st, sp=sp:
                               P.nc.tensor.matmul(p, R(l), R(r_),
                                                  start=st, stop=sp))
                    P._s0_ptt[par] = P.tick_last("tensor", "sMM")
                pendPV = mkPV
            pendPV()
            lastpv = P._s0_ptt[par]

            def vos_fn(par=par, q0=q0, b=b, lastpv=lastpv):
                w("vector", "sMM", lastpv)
                osem = f"sS{10 + par}"
                if P._st_now.get(par) is not None:
                    w("vector", osem, P._st_now[par])
                vos = P.emit("vector", lambda e, o=osl[par],
                             p=P.ps[6 + par][:, 0:128]: e.tensor_copy(o, p))
                P._s0_pv[par] = vos
                w("gpsimd", "sEV", vos)
                gtok = b * S + q0
                slab = gtok // 1024
                col = gtok % 1024
                d("gpsimd", P.cc2ai[slab, 0:128, col:col + 128], osl[par],
                  osem)
                if "attn" in P.debug:
                    dao = P.dbg["dao"].rearrange("(s f) t -> s f t", f=512)
                    d("gpsimd", dao[slab, 0:128, col:col + 128], osl[par],
                      osem)
                P._st_now[par] = P.ctrs[osem].v
            P._s0_vpend = vos_fn
        P._s0_tpend = mkT
    P._s0_tpend()
    P._s0_tpend = None
    if P._s0_vpend is not None:
        P._s0_vpend()
        P._s0_vpend = None

def _emit_norm(P, arK, arQ):
    d, w = P.dma, P.wait
    R = P.R
    ep = [P.EP[:, 512 * i:512 * (i + 1)] for i in range(8)]
    job_i, b, hl, J, cw, q0, outb, vli, stsl = P._norm_pend
    P._norm_pend = None
    li = P.row_li[J % 2][0:1, 0:cw]
    w("tensor", "sEV", vli)
    P.emit_raw("tensor", lambda e, o=P.ps[outb][:, 0:cw], li=li:
               P.nc.tensor.matmul(o, R(P.sv_ones[64:65, 0:128]), R(li),
                                  start=True, stop=True))
    vlb = P.tick_last("tensor", "sMM")
    P._norm_vlb[J % 2] = vlb
    w("vector", "sMM", vlb)
    os_ = ep[4 + J % 2]
    osem = f"sS{10 + J % 2}"
    if P._st_now.get(J % 2) is not None:
        w("vector", osem, P._st_now[J % 2])
    vn = P.emit("vector", lambda e, o=os_[:, 0:cw], a=stsl,
                p=P.ps[outb][:, 0:cw]: e.tensor_mul(o, a, p))
    P._norm_done[J % 2] = vn
    w("gpsimd", "sEV", vn)
    gtok = b * S + q0
    slab = gtok // 1024
    col = gtok % 1024
    tgt = P.cc2ai if hl < 2 else P.cc2bi
    d("gpsimd", tgt[slab, 128 * (hl % 2):128 * (hl % 2) + 128, col:col + cw],
      os_[:, 0:cw], osem)
    if "attn" in P.debug:
        dao = P.dbg["dao"].rearrange("(s f) t -> s f t", f=512)
        d("gpsimd", dao[slab, 128 * hl:128 * (hl + 1), col:col + cw],
          os_[:, 0:cw], osem)
    P._st_now[J % 2] = P.ctrs[osem].v

def phase_p3(P):
    d, w = P.dma, P.wait
    R = P.R
    ep = [P.EP[:, 512 * i:512 * (i + 1)] for i in range(8)]
    JOB = 12288
    arK = [P.AR[:, JOB * s:JOB * s + 4096] for s in range(2)]
    arQ = [P.AR[:, JOB * s + 4096:JOB * s + 8192] for s in range(2)]
    arV = [P.AR[:, JOB * s + 8192:JOB * s + 12288] for s in range(2)]
    jobs = [(b, hl) for b in range(B) for hl in range(4)]
    w("sync", "sCC", P.cc1v_done)
    w("gpsimd", "sCC", P.ccs_done)
    w("gpsimd", *P.masks_done)
    w("tensor", "sEV", P.ctrs["sEV"].v)
    w("tensor", "sSC", P.ctrs["sSC"].v)
    w("sync", "sMM", P.ctrs["sMM"].v)
    d("sync", P.XT[:, 6656:7552], P.alib2dd[:], "sW6")
    P.p3_rows_done = ("sW6", P.ctrs["sW6"].v)
    P._av_load = {}

    def load_job(job_i):
        b, hl = jobs[job_i]
        s = job_i % 2
        sem = f"sW{3 + s}"
        vsem = f"sW{7 + s}"
        for j in range(4):
            vv = P.cc1o_v[4 * b + j].rearrange("(tt p) f -> tt p f", p=128)
            for tt in range(8):
                d("sync", arV[s][:, 128 * (8 * j + tt):128 * (8 * j + tt + 1)],
                  vv[tt, :, 128 * hl:128 * (hl + 1)], vsem)
        P._av_load[job_i] = P.ctrs[vsem].v
        if job_i == 0:
            w("sync", "sCC", P.cc1k_done)
        for j in range(4):
            d("sync", arK[s][:, 1024 * j:1024 * (j + 1)],
              P.cc1ko[4 * b + j, 128 * hl:128 * (hl + 1), :], sem)
            d("sync", arQ[s][:, 1024 * j:1024 * (j + 1)],
              P.cc1qo[4 * b + j, 128 * hl:128 * (hl + 1), :], sem)
        return P.ctrs[sem].v

    def apply_chunk(job_i, ac):
        b, hl = jobs[job_i]
        s = job_i % 2
        isk = ac < 8
        ch = ac % 8
        ar = arK[s] if isk else arQ[s]
        mu_r, rs_r = (2, 3) if isk else (0, 1)
        w_ = P.sv_lnc[:, 4 * hl + (2 if isk else 0):
                      4 * hl + (3 if isk else 1)].bitcast(F32)
        b_ = P.sv_lnc[:, 4 * hl + (3 if isk else 1):
                      4 * hl + (4 if isk else 2)].bitcast(F32)
        par = ac % 2
        mu_row = P.row_ap[par]
        rs_row = P.row_ap[2 + par]
        sl = 4 * b + ch // 2
        half = 512 * (ch % 2)
        rsem = f"sS{8 + par}"
        if P._ap_vb.get(par) is not None:
            w("gpsimd", "sMM", P._ap_vb[par])
        d("gpsimd", mu_row, P.ccso[sl, mu_r, half:half + 512], rsem)
        d("gpsimd", rs_row, P.ccso[sl, rs_r, half:half + 512], rsem)
        vrows = (rsem, P.ctrs[rsem].v)
        prev = P._ap_tick.get((job_i, ac - 1) if ac else (job_i - 1, 15))
        if prev is not None:
            w("tensor", "sEV", prev)
        w("tensor", *vrows)
        P.emit_raw("tensor", lambda e, r_=mu_row: P.nc.tensor.matmul(
            P.ps[4][:, 0:512], R(P.sv_ones[0:1, 0:128]), R(r_),
            start=True, stop=True))
        P.emit_raw("tensor", lambda e, r_=rs_row: P.nc.tensor.matmul(
            P.ps[5][:, 0:512], R(P.sv_ones[32:33, 0:128]), R(r_),
            start=True, stop=True))
        vb = P.tick_last("tensor", "sMM")
        P._ap_vb[par] = vb
        w("vector", "sMM", vb)
        if P._ap_load.get(job_i) is not None:
            w("vector", f"sW{3 + s}", P._ap_load[job_i])
            P._ap_load[job_i] = None
        x_ = ar[:, 512 * ch:512 * (ch + 1)]
        esl = P.XT[:, 512 * par:512 * (par + 1)]
        P.emit("vector", lambda e, o=esl, i=x_:
               e.scalar_tensor_tensor(o, i, 1.0, P.ps[4][:, 0:512],
                                      ALU.mult, ALU.subtract))
        P.emit("vector", lambda e, o=x_, i=esl, w__=w_:
               e.scalar_tensor_tensor(o, i, w__, P.ps[5][:, 0:512],
                                      ALU.mult, ALU.mult))
        va = P.emit("vector", lambda e, o=x_, b__=b_:
                    e.tensor_scalar_add(o, o, b__))
        P._ap_tick[(job_i, ac)] = va

    P._ap_tick = {}
    P._ap_vb = {}
    P._ap_load = {}
    P._norm_pend = None
    P._norm_done = {}
    P._norm_vlb = {}
    P._st_now = {}
    P._s0_msc = {}
    P._s0_ptt = {}
    P._s0_trc = {}
    P._s0_pv = {}
    P._s0_st = {}
    P._s0_scores = None
    P._s0_tpend = None
    P._s0_vpend = None
    P._pv_pend = None
    P.job_mm_done = {}
    st_now = {}
    pt_tick = {}
    exp_tick = {}
    ci = 0

    P._ap_load[0] = load_job(0)
    for ac in range(16):
        apply_chunk(0, ac)

    for job_i, (b, hl) in enumerate(jobs):
        s = job_i % 2
        cw = SLOT_CW[hl]; win = SLOT_WIN[hl]
        nJ = 32 if hl == 0 else S // cw
        prework = []
        if job_i + 1 < len(jobs):
            def start_next(job_n=job_i + 1, cur=job_i):
                if cur >= 1:
                    w("sync", "sMM", P.job_mm_done[cur - 1])
                P._ap_load[job_n] = load_job(job_n)
            prework.append(start_next)
            for ac in range(16):
                prework.append(lambda job_n=job_i + 1, ac=ac:
                               apply_chunk(job_n, ac))
        pw_at = {}
        if prework:
            span = max(1, nJ - 2)
            for ii, fn in enumerate(prework):
                at = 1 + (ii * span) // len(prework)
                pw_at.setdefault(min(at, nJ - 1), []).append(fn)

        w("tensor", "sEV", P._ap_tick[(job_i, 15)])
        w("tensor", f"sW{7 + job_i % 2}", P._av_load[job_i])
        stash = P.AR[:, 24576 + 4096 * (job_i % 2):24576 + 4096 * (job_i % 2)
                     + 4096]
        if hl == 0:
            _emit_slot0_job(P, job_i, b, s, arK, arQ, arV, pw_at)
            P.job_mm_done[job_i] = P.ctrs["sMM"].v
            P._s0_done = P.ctrs["sEV"].v
            P._s0_done_sc = P.ctrs["sSC"].v
            continue
        if getattr(P, "_s0_done", None) is not None:
            w("tensor", "sEV", P._s0_done)
            w("tensor", "sSC", P._s0_done_sc)
            P._s0_done = None
        if job_i == 7:
            vca, inca = P.tick("gpsimd", "sCC")
            w("gpsimd", "sS10", P.ctrs["sS10"].v)
            w("gpsimd", "sS11", P.ctrs["sS11"].v)
            P.emit_raw("gpsimd", lambda e, inc=inca: inc(_cc(e, "AllToAll",
                P.cc2a_in[:], P.cc2a_out[:])))
            P.cc2a_done = vca
        for J in range(nJ):
            for fn in pw_at.get(J, []):
                fn()
            k0 = cw * (J + 1) - 1
            kc_hi = k0 // 128
            kc_lo = max(0, (k0 + 1 - win) // 128)
            q0 = cw * J
            outb = 2 if J % 2 == 0 else 6
            lb_ = 3 if J % 2 == 0 else 7
            for kc in range(kc_lo, kc_hi + 1):
                sb_ = ci % 2
                if ci >= 2:
                    w("tensor", "sSC", exp_tick[ci - 2])
                P.emit_raw("tensor", lambda e, p=P.ps[sb_][:, 0:cw],
                           l=arK[s][:, 128 * kc:128 * (kc + 1)],
                           r_=arQ[s][:, q0:q0 + cw]:
                           P.nc.tensor.matmul(p, R(l), R(r_),
                                              start=True, stop=True))
                vqk = P.tick_last("tensor", "sMM")
                if kc == kc_lo + 1 and P._norm_pend is not None:
                    _emit_norm(P, arK, arQ)
                shift = k0 - 128 * kc
                ji = (shift - 63) // 64
                assert (shift - 63) % 64 == 0 and 0 <= ji < 67, (shift, ji)
                pts = ep[ci % 4]
                w("scalar", "sMM", vqk)
                if ci >= 4:
                    w("scalar", "sMM", pt_tick[ci - 4])
                vex = P.emit("scalar", lambda e, o=pts[:, 0:cw],
                             p=P.ps[sb_][:, 0:cw],
                             bias=P.sv_alib[:, 67 * hl + ji:67 * hl + ji + 1]:
                             e.activation(out=o, in_=p, func=AF.Exp,
                                          bias=bias, scale=SCALE))
                exp_tick[ci] = vex
                ptready = ("sSC", vex)
                sshift = 128 * kc - q0
                if sshift > -128:
                    midx = sshift // 64 + 1
                    assert 0 <= midx < 9, (sshift, midx)
                    w("vector", "sSC", vex)
                    vm = P.emit("vector", lambda e, o=pts[:, 0:cw],
                                m=P.sv_masks[:, 512 * midx:512 * midx + cw]:
                                e.tensor_mul(o, o, m))
                    ptready = ("sEV", vm)
                if P._pv_pend is not None:
                    P._pv_pend()
                def mkC(ci=ci, pts=pts, ptready=ptready,
                        first=(kc == kc_lo), sp=(kc == kc_hi), kc=kc,
                        outb=outb, lb_=lb_, cw=cw, Jp=J % 2, s=s):
                    if first and P._norm_done.get(Jp) is not None:
                        w("tensor", "sEV", P._norm_done[Jp])
                    w("tensor", *ptready)
                    P.emit_raw("tensor", lambda e, p=P.ps[outb][:, 0:cw],
                               l=arV[s][:, 128 * kc:128 * (kc + 1)],
                               r_=pts[:, 0:cw], st=first, sp=sp:
                               P.nc.tensor.matmul(p, R(l), R(r_),
                                                  start=st, stop=sp))
                    P.emit_raw("tensor", lambda e, p=P.ps[lb_][0:1, 0:cw],
                               r_=pts[:, 0:cw], st=first, sp=sp:
                               P.nc.tensor.matmul(p, R(P.sv_ones[:, 0:1]),
                                                  R(r_), start=st, stop=sp))
                    pt_tick[ci] = P.tick_last("tensor", "sMM")
                P._pv_pend = mkC
                ci += 1
            P._pv_pend()
            P._pv_pend = None
            w("vector", "sMM", pt_tick[ci - 1])
            P.emit("vector", lambda e, o=stash[:, q0:q0 + cw],
                   p=P.ps[outb][:, 0:cw]: e.tensor_copy(o, p))
            li = P.row_li[J % 2][0:1, 0:cw]
            if P._norm_vlb.get(J % 2) is not None:
                w("vector", "sMM", P._norm_vlb[J % 2])
            P.emit("vector", lambda e, o=li, p=P.ps[lb_][0:1, 0:cw]:
                   e.tensor_scalar_max(o, p, 1e-30))
            vli = P.emit("vector", lambda e, o=li: e.reciprocal(o, o))
            if P._norm_pend is not None:
                _emit_norm(P, arK, arQ)
            P._norm_pend = (job_i, b, hl, J, cw, q0, outb, vli,
                            stash[:, q0:q0 + cw])
        P.job_mm_done[job_i] = pt_tick[ci - 1]
        if job_i == len(jobs) - 1:
            _emit_norm(P, arK, arQ)
    P.p3_store_total =    P.p3_store_total = (P.ctrs["sS10"].v, P.ctrs["sS11"].v)

def phase_p4(P):
    w, t = P.wait, P.tick
    w("gpsimd", "sS10", P.p3_store_total[0])
    w("gpsimd", "sS11", P.p3_store_total[1])
    v1, inc1 = t("gpsimd", "sCC")
    P.emit_raw("gpsimd", lambda e, inc=inc1: inc(_cc(e, "AllToAll",
        P.cc2b_in[:], P.cc2b_out[:])))
    P.cc2b_done = v1

P5SEQ = [(j_, h_) for h_ in (0, 1) for j_ in range(8)] + \
        [(j_, h_) for h_ in (2, 3) for j_ in range(8)]

def phase_p56(P, which):
    d, w = P.dma, P.wait
    R = P.R
    P._trev = {}
    P._trgp = {}
    ep = [P.EP[:, 512 * i:512 * (i + 1)] for i in range(8)]
    is5 = which == 5
    wsrc = P.woutTp if is5 else P.bwT
    afm = [P.AR[:, 512 * i:512 * (i + 1)] for i in range(NDT)]
    stg = [P.AR[:, 16384 + 4096 * tt:16384 + 4096 * (tt + 1)]
           for tt in range(4)]
    junkof = [P.AR[:, 4096 * i:4096 * (i + 1)] for i in range(4)]
    STA = P.col_small
    hst = [P.WRG[:, 5120 + 256 * tt:5120 + 256 * (tt + 1)] for tt in range(4)]
    wslot = [P.WRG[:, 512 * i:512 * (i + 1)] for i in range(10)]
    bank = {}
    bank_sc = {}
    wtick = {}
    gtick = {}
    hst_tick = {}
    w("sync", "sMM", P.ctrs["sMM"].v)
    w("sync", "sEV", P.ctrs["sEV"].v)
    w("tensor", "sEV", P.ctrs["sEV"].v)
    w("tensor", "sSC", P.ctrs["sSC"].v)
    w("tensor", "sGE", P.ctrs["sGE"].v)
    if not is5:
        for ssem, sval in P.p5_store_total.items():
            w("sync", ssem, sval)
    widx = 0
    gwidx = 0
    for tsup in range(2):
        tks = slice(512 * tsup, 512 * (tsup + 1))
        if tsup == 1:
            w("sync", "sMM", P.ctrs["sMM"].v)
            w("sync", "sEV", P.ctrs["sEV"].v)
        vload16 = None
        if is5:
            if tsup == 0:
                w("sync", "sCC", P.cc2a_done)
            for i_, (j_, h_) in enumerate(P5SEQ):
                if i_ == 16:
                    vload16 = P.ctrs["sIN"].v
                    if tsup == 0:
                        w("sync", "sCC", P.cc2b_done)
                src2 = P.cc2ao if h_ < 2 else P.cc2bo
                d("sync", afm[i_],
                  src2[j_, 128 * (h_ % 2):128 * (h_ % 2) + 128, tks],
                  "sIN" if i_ < 16 else "sOUT")
            vload = P.ctrs["sOUT"].v
        else:
            for dt_ in range(NDT):
                d("sync", afm[dt_],
                  P.ln3sc[128 * dt_:128 * (dt_ + 1), tks], "sIN")
            vload = P.ctrs["sIN"].v
        w("tensor", "sIN", vload16 if vload16 is not None else vload)

        if not is5:
            for bb in (4, 5, 7):
                if bank.get(bb) is not None:
                    w("tensor", "sEV", bank[bb])
                    w("tensor", "sSC", bank_sc[bb])
            for tt in range(4):
                for dd in range(NDT):
                    sl = gwidx % 10
                    sem = f"sW{sl}"
                    if gwidx >= 10:
                        w("sync", "sMM", gtick[gwidx - 10])
                    d("sync", wslot[sl][:, 0:252],
                      P.glw[128 * dd:128 * (dd + 1), :], sem)
                    w("tensor", sem, P.ctrs[sem].v)
                    P.emit_raw("tensor", lambda e, p=P.ps[7][:, 0:28],
                               l=afm[dd][:, 128 * tt:128 * (tt + 1)],
                               r_=wslot[sl][:, 0:28],
                               st=(dd == 0), sp=(dd == NDT - 1):
                               P.nc.tensor.matmul(p, R(l), R(r_), start=st,
                                                  stop=sp))
                    P.emit_raw("tensor", lambda e, p=P.ps[4][:, 0:224],
                               l=afm[dd][:, 128 * tt:128 * (tt + 1)],
                               r_=wslot[sl][:, 28:252],
                               st=(dd == 0), sp=(dd == NDT - 1):
                               P.nc.tensor.matmul(p, R(l), R(r_), start=st,
                                                  stop=sp))
                    gtick[gwidx] = P.tick_last("tensor", "sMM")
                    gwidx += 1
                vg = gtick[gwidx - 1]
                w("vector", "sMM", vg)
                lg = P.ps[7][:, 0:28]
                m1 = STA[:, 96:97].bitcast(F32)
                m2 = STA[:, 97:98].bitcast(F32)
                p1 = STA[:, 98:99].bitcast(F32)
                p2 = STA[:, 99:100].bitcast(F32)
                dl = STA[:, 100:101].bitcast(F32)
                eq1 = ep[0][:, 0:28]; masked = ep[0][:, 32:60]
                eq2 = ep[0][:, 64:92]; gate = ep[0][:, 96:124]
                gexp = ep[1][:, 0:224]
                P.emit("vector", lambda e: e.reduce_max(m1, lg, axis=AX.X))
                P.emit("vector", lambda e: e.tensor_scalar(
                    eq1, lg, m1, None, ALU.is_ge))
                P.emit("vector", lambda e: e.scalar_tensor_tensor(
                    masked, eq1, -1e30, lg, ALU.mult, ALU.add))
                P.emit("vector", lambda e: e.reduce_max(m2, masked, axis=AX.X))
                vd = P.emit("vector", lambda e: e.tensor_sub(dl, m1, m2))
                P.emit("vector", lambda e: e.tensor_scalar(
                    eq2, masked, m2, None, ALU.is_ge))
                w("scalar", "sEV", vd)
                vsg = P.emit("scalar", lambda e: e.activation(
                    out=p1, in_=dl, func=AF.Sigmoid))
                w("vector", "sSC", vsg)
                P.emit("vector", lambda e: e.tensor_scalar_mul(p1, p1, 2.0))
                P.emit("vector", lambda e: e.tensor_scalar(
                    p2, p1, -1.0, 2.0, ALU.mult, ALU.add))
                P.emit("vector", lambda e: e.tensor_scalar_mul(gate, eq1, p1))
                P.emit("vector", lambda e: e.scalar_tensor_tensor(
                    gate, eq2, p2, gate, ALU.mult, ALU.add))
                ge3 = gexp.rearrange("p (e r) -> p e r", r=8)
                for r_ in range(8):
                    P.emit("vector", lambda e, r_=r_: e.tensor_copy(
                        ge3[:, :, r_], gate))
                vhs = P.emit("vector", lambda e: e.tensor_mul(
                    ep[2][:, 0:224], P.ps[4][:, 0:224], gexp))
                w("tensor", "sEV", vhs)
                P.emit_raw("tensor", lambda e: P.nc.tensor.transpose(
                    P.ps[5][:, 0:128], ep[2][:, 0:128].bitcast(F32),
                    P.sv_ident.bitcast(F32)))
                P.emit_raw("tensor", lambda e: P.nc.tensor.transpose(
                    P.ps[5][0:96, 128:256], ep[2][:, 128:224].bitcast(F32),
                    P.sv_ident.bitcast(F32)))
                vt1 = P.tick_last("tensor", "sMM")
                w("vector", "sMM", vt1)
                P.emit("vector", lambda e, tt=tt: e.tensor_copy(
                    hst[tt][:, 0:128], P.ps[5][:, 0:128]))
                vev = P.emit("vector", lambda e, tt=tt: e.tensor_copy(
                    hst[tt][0:96, 128:256], P.ps[5][0:96, 128:256]))
                hst_tick[tt] = vev

        if not is5:
            P._gate_done = P.ctrs["sEV"].v
            w("sync", "sMM", P.ctrs["sMM"].v)
        for pp in (8, 9):
            if P.ctrs[f"sS{pp}"].v:
                w("vector", f"sS{pp}", P.ctrs[f"sS{pp}"].v)
        for ec in range(8):
            pbank = 4 * (ec % 2)
            ntile = NDT + (0 if is5 else 2)
            for dd in range(ntile):
                gw = widx
                sl = gw % 10
                sem = f"sW{sl}"
                if gw >= 10:
                    w("sync", "sMM", wtick[gw - 10])
                if dd < NDT:
                    dt2 = (4 * P5SEQ[dd][0] + P5SEQ[dd][1]) if is5 else dd
                    blk = wsrc[128 * dt2:128 * (dt2 + 1),
                               512 * ec:512 * (ec + 1)]
                    d("sync", wslot[sl], blk, sem)
                elif dd == NDT:
                    d("sync", wslot[sl],
                      P.lbR[0:128, 512 * ec:512 * (ec + 1)], sem)
                else:
                    d("sync", wslot[sl][0:96, :],
                      P.lbR[128:224, 512 * ec:512 * (ec + 1)], sem)
                widx += 1
                w("tensor", sem, P.ctrs[sem].v)
                rhs = wslot[sl]
                if dd == 0 and ec == 1:
                    if not is5:
                        w("tensor", "sEV", P._gate_done)
                    elif tsup == 1:
                        for v_ in list(P._trev.values()):
                            w("tensor", "sEV", v_)
                if dd == 16 and ec == 0 and is5:
                    w("tensor", "sOUT", vload)
                if dd < NDT:
                    for tt in range(4):
                        if dd == 0 and bank.get(pbank + tt) is not None:
                            w("tensor", "sEV", bank[pbank + tt])
                            w("tensor", "sSC", bank_sc[pbank + tt])
                        P.emit_raw("tensor",
                                   lambda e, p=P.ps[pbank + tt][:, 0:512],
                                   l=afm[dd][:, 128 * tt:128 * (tt + 1)],
                                   r_=rhs, st=(dd == 0),
                                   sp=(is5 and dd == NDT - 1):
                                   P.nc.tensor.matmul(p, R(l), R(r_),
                                                      start=st, stop=sp))
                elif dd == NDT:
                    for tt in range(4):
                        w("tensor", "sEV", hst_tick[tt])
                        P.emit_raw("tensor",
                                   lambda e, p=P.ps[pbank + tt][:, 0:512],
                                   l=hst[tt][:, 0:128], r_=rhs:
                                   P.nc.tensor.matmul(p, R(l), R(r_),
                                                      start=False, stop=False))
                else:
                    for tt in range(4):
                        P.emit_raw("tensor",
                                   lambda e, p=P.ps[pbank + tt][:, 0:512],
                                   l=hst[tt][0:96, 128:256], r_=rhs[0:96, :]:
                                   P.nc.tensor.matmul(p, R(l), R(r_),
                                                      start=False, stop=True))
                wtick[gw] = P.tick_last("tensor", "sMM")
            vmm = wtick[widx - 1]
            w("scalar", "sMM", vmm)
            w("vector", "sMM", vmm)
            for tt in range(4):
                vsc = P.emit("scalar", lambda e,
                             o=stg[tt][:, 512 * ec:512 * (ec + 1)],
                             p=P.ps[pbank + tt][:, 0:512],
                             a=STA[:, 8 * tt + ec:8 * tt + ec + 1]:
                             e.activation(out=o, in_=p, func=AF.Copy,
                                          accum_out=a))
                w("vector", "sSC", vsc)
                vvc = P.emit("vector", lambda e, o=ep[2 + tt][:, 0:512],
                             p=stg[tt][:, 512 * ec:512 * (ec + 1)],
                             a=STA[:, 32 + 8 * tt + ec:32 + 8 * tt + ec + 1]:
                             e.scalar_tensor_tensor(
                                 o, p, 1.0, p, ALU.mult, ALU.mult,
                                 accum_out=a))
                bank[pbank + tt] = vvc
                bank_sc[pbank + tt] = vsc

        for tt in range(4):
            tglob = 512 * tsup + 128 * tt
            mu_ = STA[:, 64 + 8 * tt:65 + 8 * tt].bitcast(F32)
            rs_ = STA[:, 65 + 8 * tt:66 + 8 * tt].bitcast(F32)
            t1 = STA[:, 66 + 8 * tt:67 + 8 * tt].bitcast(F32)
            t2 = STA[:, 67 + 8 * tt:68 + 8 * tt].bitcast(F32)
            t3 = STA[:, 68 + 8 * tt:69 + 8 * tt].bitcast(F32)
            t4 = STA[:, 69 + 8 * tt:70 + 8 * tt].bitcast(F32)
            w("vector", "sSC", max(bank_sc[tt], bank_sc[4 + tt]))
            P.emit("vector", lambda e, o=t1, i=STA[:, 8 * tt:8 * tt + 8]:
                   e.reduce_sum(o, i, axis=AX.X))
            P.emit("vector", lambda e, o=mu_, i=t1:
                   e.tensor_scalar_mul(o, i, 1.0 / D))
            P.emit("vector", lambda e, o=t2,
                   i=STA[:, 32 + 8 * tt:32 + 8 * tt + 8]:
                   e.reduce_sum(o, i, axis=AX.X))
            P.emit("vector", lambda e, o=t2, i=t2:
                   e.tensor_scalar_mul(o, i, 1.0 / D))
            P.emit("vector", lambda e, o=t1, m=mu_: e.tensor_mul(o, m, m))
            P.emit("vector", lambda e, o=t2, a=t2, b=t1: e.tensor_sub(o, a, b))
            P.emit("vector", lambda e, o=t2: e.tensor_scalar_add(o, o, EPS))
            vrc = P.emit("vector", lambda e, o=t2: e.reciprocal(o, o))
            w("scalar", "sEV", vrc)
            vsq = P.emit("scalar", lambda e, o=rs_, i=t2:
                         e.activation(out=o, in_=i, func=AF.Sqrt))
            P.emit("vector", lambda e, o=mu_: e.tensor_scalar_mul(o, o, -1.0))
            w("vector", "sSC", vsq)
            xslot = P.XT[:, 0:4096] if tt % 2 == 0 else P.XT[:, 4096:8192]
            res_src = P.xtm if is5 else P.x2sc
            sem = "sXT0" if tt % 2 == 0 else "sXT1"
            if P._p56_xt.get(tt % 2) is not None:
                ssem, sval, vev_ = P._p56_xt[tt % 2]
                w("sync", ssem, sval)
                w("sync", "sEV", vev_)
            d("sync", xslot, res_src[tglob:tglob + 128, :], sem)
            w("vector", sem, P.ctrs[sem].v)
            P.emit("vector", lambda e, o=stg[tt], m=mu_, r_=rs_:
                   e.tensor_scalar(o, o, m, r_, ALU.add, ALU.mult))
            vap = P.emit("vector", lambda e, o=xslot, a=stg[tt], b=xslot:
                         e.tensor_add(o, a, b))
            w("gpsimd", "sEV", vap)
            if is5:
                xsem = f"sS{10 + tt % 2}"
                d("gpsimd", P.x2sc[tglob:tglob + 128, :], xslot, xsem)
                if "x2" in P.debug:
                    d("gpsimd", P.dbg["dx2"][tglob:tglob + 128, :], xslot,
                      xsem)
                _x2st = (xsem, P.ctrs[xsem].v)
                P.emit("vector", lambda e, o=t3, i=xslot:
                       e.reduce_sum(o, i, axis=AX.X))
                P.emit("vector", lambda e, o=t3: e.tensor_scalar_mul(
                    o, o, 1.0 / D))
                P.emit("vector", lambda e, o=junkof[tt], i=xslot, a=t4:
                       e.scalar_tensor_tensor(o, i, 1.0, i, ALU.mult,
                                              ALU.mult, accum_out=a))
                P.emit("vector", lambda e, o=t4: e.tensor_scalar_mul(
                    o, o, 1.0 / D))
                P.emit("vector", lambda e, o=t1, m=t3: e.tensor_mul(o, m, m))
                P.emit("vector", lambda e, o=t4, a=t4, b=t1:
                       e.tensor_sub(o, a, b))
                P.emit("vector", lambda e, o=t4: e.tensor_scalar_add(
                    o, o, EPS))
                vr2 = P.emit("vector", lambda e, o=t4: e.reciprocal(o, o))
                w("scalar", "sEV", vr2)
                vq2 = P.emit("scalar", lambda e, o=t1, i=t4:
                             e.activation(out=o, in_=i, func=AF.Sqrt))
                P.emit("vector", lambda e, o=t3: e.tensor_scalar_mul(
                    o, o, -1.0))
                w("vector", "sSC", vq2)
                vln3 = P.emit("vector", lambda e, o=stg[tt], i=xslot, m=t3,
                              r_=t1: e.tensor_scalar(o, i, m, r_, ALU.add,
                                                     ALU.mult))
                w("tensor", "sEV", vln3)
                if tt == 0:
                    for bb in (5, 6):
                        if bank.get(bb) is not None:
                            w("tensor", "sEV", bank[bb])
                            w("tensor", "sSC", bank_sc[bb])
                trev = P._trev
                trgp = P._trgp
                for blk in range(NDT):
                    pb = 5 if blk % 2 == 0 else 6
                    if trev.get(blk % 2) is not None:
                        w("tensor", "sEV", trev[blk % 2])
                    P.emit_raw("tensor", lambda e, p=P.ps[pb][:, 0:128],
                               i=stg[tt][:, 128 * blk:128 * (blk + 1)]:
                               P.nc.tensor.transpose(p, i.bitcast(F32),
                                                     P.sv_ident.bitcast(F32)))
                    vtr = P.tick_last("tensor", "sMM")
                    w("vector", "sMM", vtr)
                    if trgp.get(blk % 2) is not None:
                        w("vector", *trgp[blk % 2])
                    vte = P.emit("vector", lambda e,
                                 o=ep[2 + blk % 2][:, 0:128],
                                 p=P.ps[pb][:, 0:128]: e.tensor_copy(o, p))
                    trev[blk % 2] = vte
                    w("gpsimd", "sEV", vte)
                    tsem = f"sS{8 + blk % 2}"
                    d("gpsimd", P.ln3sc[128 * blk:128 * (blk + 1),
                                        tglob:tglob + 128],
                      ep[2 + blk % 2][:, 0:128], tsem)
                    trgp[blk % 2] = (tsem, P.ctrs[tsem].v)
                P._p56_xt[tt % 2] = (_x2st[0], _x2st[1], vln3)
            else:
                ysem = f"sS{10 + tt % 2}"
                d("gpsimd", P.y[tglob:tglob + 128, :], xslot, ysem)
                P._p56_xt[tt % 2] = (ysem, P.ctrs[ysem].v, vap)
    if is5:
        P.p5_store_total = {s: P.ctrs[s].v
                            for s in ("sS8", "sS9", "sS10", "sS11")}

def build(debug=None, stop_after=None):
    P = Prog(debug)
    P._p56_xt = {}
    phase_p0(P)
    phase_p1(P)
    if stop_after == "p1":
        for i in range(12):
            if P.ctrs[f"sS{i}"].v:
                P.wait("gpsimd", f"sS{i}", P.ctrs[f"sS{i}"].v)
        P.wait("gpsimd", "sST", P.ctrs["sST"].v)
        return P
    phase_p2(P)
    phase_p3(P)
    if stop_after == "p3":
        for i in range(12):
            if P.ctrs[f"sS{i}"].v:
                P.wait("gpsimd", f"sS{i}", P.ctrs[f"sS{i}"].v)
        return P
    phase_p4(P)
    phase_p56(P, 5)
    if stop_after == "p5":
        for i in range(12):
            if P.ctrs[f"sS{i}"].v:
                P.wait("gpsimd", f"sS{i}", P.ctrs[f"sS{i}"].v)
        return P
    phase_p56(P, 6)
    P.wait("gpsimd", "sS10", P.ctrs["sS10"].v)
    P.wait("gpsimd", "sS11", P.ctrs["sS11"].v)
    return P

def kernel(**inputs):
    per_core = stage_inputs(inputs)
    P = build()
    nc = P.finalize()
    res = run_bass_kernel_spmd(nc, per_core, core_ids=list(range(NCORE)))
    out = np.concatenate([res.results[c]["y"] for c in range(NCORE)], axis=0)
    return out.reshape(B, S, D)
